# version 24
# baseline (speedup 1.0000x reference)
import numpy as np
import ml_dtypes
import concourse.bass as bass
import concourse.mybir as mybir
from concourse.bass_utils import run_bass_kernel_spmd

BF = ml_dtypes.bfloat16
F32 = mybir.dt.float32
BF16 = mybir.dt.bfloat16
I32 = mybir.dt.int32
AF = mybir.ActivationFunctionType
ALU = mybir.AluOpType
AX = mybir.AxisListType

NL, D, H, HD, E = 1024, 2048, 16, 128, 6144
NT, NC, NE, NG = 8, 16, 48, 32
SCALE = float(HD) ** -0.5
EPS = 1e-5
GUARD = 1e-30
Q = 1.0 / 127.0
MAGIC = 12582912.0
INV_D = 1.0 / D

PHASES = 5


def build_program():
    nc = bass.Bass("TRN2", target_bir_lowering=False)

    d_xatt = nc.dram_tensor("x_att", [NL, D], F32, kind="ExternalInput").ap()
    d_xres = nc.dram_tensor("x_res", [NL, D], F32, kind="ExternalInput").ap()
    d_wqkvT = nc.dram_tensor("wqkvT", [D, E], F32, kind="ExternalInput").ap()
    d_wpT = nc.dram_tensor("wpT", [D, D], F32, kind="ExternalInput").ap()
    d_wkT = nc.dram_tensor("wkT", [D, D], F32, kind="ExternalInput").ap()
    d_bias = nc.dram_tensor("bias", [1, D], F32, kind="ExternalInput").ap()
    d_ident = nc.dram_tensor("ident", [128, 128], BF16, kind="ExternalInput").ap()
    d_mask = nc.dram_tensor("mask", [128, 512], BF16, kind="ExternalInput").ap()
    d_out = nc.dram_tensor("out", [NL, D], F32, kind="ExternalOutput").ap()

    BASE = 16640

    def A(name, shape, dt, offset):
        return nc.alloc_sbuf_tensor_at(name, shape, dt, offset=BASE + offset)

    aT = A("aT", [128, 16384], BF16, offset=0)
    xr = A("xr", [128, 16384], F32, offset=0)
    qT = A("qT", [128, 16384], BF16, offset=65536)
    kT = A("kT", [128, 16384], BF16, offset=98304)
    vT = A("vT", [128, 16384], BF16, offset=131072)
    xbuf = [A(f"xbuf{i}", [128, 2048], F32, offset=65536 + 8192 * i) for i in range(2)]
    abuf = [A(f"abuf{i}", [128, 2048], BF16, offset=81920 + 4096 * i) for i in range(2)]
    junkA = A("junkA", [128, 2048], F32, offset=90112)
    qk = A("qk", [128, 32768], BF16, offset=65536)
    wpb = A("wpb", [128, 16384], BF16, offset=131072)
    xqs = [A(f"xqs{i}", [128, 2048], BF16, offset=131072 + 4096 * i) for i in range(2)]
    itile = A("itile", [128, 2048], I32, offset=139264)
    bqi = A("bqi", [128, 2048], I32, offset=139264)
    junkG2 = A("junkG2", [128, 2048], F32, offset=147456)
    ttile = A("ttile", [128, 2048], F32, offset=155648)
    bias_b = A("bias_b", [128, 2048], F32, offset=155648)
    sT = A("sT", [128, 16384], BF16, offset=163840)
    xu2 = A("xu2", [128, 2048], F32, offset=163840)
    xi32 = A("xi32", [128, 2048], I32, offset=172032)
    qx = A("qx", [128, 2048], BF16, offset=180224)
    bqb = A("bqb", [128, 2048], F32, offset=184320)
    wqb = [A(f"wqb{i}", [128, 2048], BF16, offset=198240 + 4096 * i) for i in range(3)]
    Ebuf = [A(f"Ebuf{i}", [128, 512], BF16, offset=0 + 1024 * i) for i in range(2)]
    Embuf = [A(f"Embuf{i}", [128, 512], BF16, offset=2048 + 1024 * i) for i in range(2)]
    Abuf = [A(f"Abuf{i}", [128, 512], BF16, offset=4096 + 1024 * i) for i in range(2)]
    aTTbuf = [A(f"aTTbuf{i}", [128, 512], BF16, offset=6144 + 1024 * i) for i in range(2)]
    V8buf = [A(f"V8buf{i}", [128, 512], BF16, offset=8192 + 1024 * i) for i in range(2)]
    ktb = [A(f"ktb{i}", [128, 1024], F32, offset=198240 + 4096 * i) for i in range(2)]
    ktc = [A(f"ktc{i}", [128, 1024], F32, offset=32768 + 4096 * i) for i in range(2)]
    kti = [A(f"kti{i}", [128, 1024], I32, offset=198240 + 4096 * i) for i in range(2)]
    kqb = [A(f"kqb{i}", [128, 1024], BF16, offset=206432 + 2048 * i) for i in range(3)]
    ident = A("ident", [128, 128], BF16, offset=196608)
    mask4 = A("mask4", [128, 512], BF16, offset=196864)
    cols = A("cols", [128, 88], F32, offset=197888)
    bnbuf = A("bnbuf", [128, 24], F32, offset=212608)
    mn0buf = A("mn0buf", [128, 8], F32, offset=212576)

    def C_S4(p):
        return cols[:, 32 + 4 * p:36 + 4 * p]

    def C_R4(p):
        return cols[:, 40 + 4 * p:44 + 4 * p]

    pF = [nc.alloc_psum_tensor(f"pF{i}", [128, 512], F32) for i in range(4)]
    pH = [nc.alloc_psum_tensor(f"pH{i}", [128, 512], BF16) for i in range(4)]

    sems, cnt = {}, {}

    def S_(name):
        if name not in sems:
            sems[name] = nc.semaphore(name).__enter__()
            cnt[name] = 0
        return sems[name]

    def inc(instr, name, k=1):
        h = S_(name)
        instr.then_inc(h, k)
        cnt[name] += k
        return (name, cnt[name])

    def wt(eng, handle):
        eng.wait_ge(S_(handle[0]), handle[1])

    def col(i):
        return cols[:, i:i + 1]

    def ecopy(eng, dst, srcap):
        if hasattr(eng, "tensor_copy"):
            return eng.tensor_copy(dst, srcap)
        return eng.copy(dst, srcap)

    h_sk = {}
    with nc.Block() as blk:
        def _bEvac(eng, nb, semname, extra=None):
            for e in range(NE):
                if extra is not None and 8 <= e < 40:
                    extra(eng, e - 8)
                eng.wait_ge(S_("bMM"), e + 1)
                sec, head = e // 16, e % 16
                dstT = (qT, kT, vT)[sec]
                dst = dstT.rearrange("p (m b t i) -> p i b t m",
                                     m=16, b=8, t=8, i=16)[
                    :, head, 4 * nb:4 * nb + 4, :, :]
                src = pF[2 * (e % 2) + nb][:, :].rearrange(
                    "p (b t m) -> p b t m", b=4, t=8, m=16)
                inc(ecopy(eng, dst, src), semname)

        def _p1unit(v, i):
            ft, hh = i // 2, i % 2
            v.wait_ge(S_(f"fKt1{i % 2}"), 16 * (i // 2 + 1))
            h1 = inc(v.reduce_max(col(84 + hh), ktc[i % 2][:, :], AX.X,
                                  apply_absolute_value=True), "fK1")
            if hh == 1:
                wt(v, h1)
                h_cb = inc(v.tensor_tensor(col(84), col(84), col(85),
                                           ALU.max), "selfV")
                wt(v, h_cb)
                h_s = inc(v.tensor_scalar(cols[:, 48 + ft:49 + ft],
                                          col(84), Q, GUARD, ALU.mult,
                                          ALU.max), "selfV")
                wt(v, h_s)
                h_sk[ft] = inc(v.reciprocal(cols[:, 64 + ft:65 + ft],
                                            cols[:, 48 + ft:49 + ft]),
                               "selfV")

        @blk.sync
        def _(sync):
            inc(sync.dma_start(ident[:, :], d_ident[:, :]), "aCn", 16)
            inc(sync.dma_start(mask4[:, :], d_mask[:, :]), "aCn", 16)
            for t in range(NT):
                if t >= 2:
                    sync.wait_ge(S_("aVn"), t - 1)
                    sync.wait_ge(S_("aVv"), t - 1)
                inc(sync.dma_start(xbuf[t % 2][:, :],
                                   d_xatt[128 * t:128 * (t + 1), :]),
                    f"aLd{t % 2}", 16)
            for idx in range(2 * NC):
                ft, hh = idx // 2, idx % 2
                if idx >= 2:
                    sync.wait_ge(S_("fK1"), idx - 1)
                inc(sync.dma_start(ktc[idx % 2][:, :],
                                   d_wkT[128 * ft:128 * (ft + 1),
                                         1024 * hh:1024 * (hh + 1)]),
                    f"fKt1{idx % 2}", 16)

        def _aEvac(s, t):
            for bb in range(4):
                bq = 4 * t + bb
                s.wait_ge(S_("aTt"), bq + 1)
                dst = aT.rearrange("p (c n) -> p c n", c=NC)[
                    :, 4 * bb:4 * bb + 4, 128 * t:128 * (t + 1)]
                src = pH[bq % 4][:, :].rearrange("p (c n) -> p c n", c=4)
                inc(ecopy(s, dst, src), "aEv")

        @blk.scalar
        def _(s):
            for t in range(NT):
                b = 16 * (t % 2)
                s.wait_ge(S_("aVv"), t + 1)
                h_ln = inc(s.activation(col(b + 2), col(b + 4), AF.Ln), "selfS")
                wt(s, h_ln)
                h_rs = inc(s.activation(col(b + 5), col(b + 2), AF.Exp,
                                        scale=-0.5), "aSr")
                wt(s, h_rs)
                h_b1 = inc(s.mul(col(b + 6), col(b + 3), col(b + 5)), "selfS")
                wt(s, h_b1)
                h_b2 = inc(s.activation(col(b + 6), col(b + 6), AF.Identity,
                                        scale=-1.0), "selfS")
                wt(s, h_b2)
                if t >= 2:
                    s.wait_ge(S_("aTt"), 4 * t - 4)
                inc(s.activation(abuf[t % 2][:, :], xbuf[t % 2][:, :], AF.Identity,
                                 bias=col(b + 6), scale=col(b + 5)), "aVn")
                if t >= 1:
                    _aEvac(s, t - 1)
            _aEvac(s, NT - 1)
            _bEvac(s, 1, "bEvS")

        @blk.vector
        def _(v):
            for t in range(NT):
                v.wait_ge(S_(f"aLd{t % 2}"), 16 * (t // 2 + 1))
                b = 16 * (t % 2)
                xbt = xbuf[t % 2]
                h_bn = None
                for q4 in range(4):
                    h_bn = inc(v.bn_stats(bnbuf[:, 6 * q4:6 * q4 + 6],
                                          xbt[:, 512 * q4:512 * (q4 + 1)]),
                               "selfV")
                wt(v, h_bn)
                mv = bnbuf[:, 0:24].rearrange("p (g c) -> p c g", g=8, c=3)
                h_sm = inc(v.reduce_sum(col(b + 3), mv[:, 1, :], AX.X), "selfV")
                h_sq = inc(v.tensor_tensor(cols[:, 80:88], mv[:, 1, :],
                                           mv[:, 1, :], ALU.mult), "selfV")
                h_sv = inc(v.reduce_sum(col(b + 2), mv[:, 2, :], AX.X), "selfV")
                wt(v, h_sq)
                h_s2 = inc(v.reduce_sum(col(b + 4), cols[:, 80:88], AX.X),
                           "selfV")
                wt(v, h_sv)
                h_cv = inc(v.tensor_scalar(col(b + 2), col(b + 2), 1.0 / 2048,
                                           EPS, ALU.mult, ALU.add), "selfV")
                wt(v, h_s2)
                wt(v, h_cv)
                h_ex = inc(v.scalar_tensor_tensor(col(b + 4), col(b + 4), 0.125,
                                                  col(b + 2), ALU.mult, ALU.add),
                           "selfV")
                wt(v, h_sm)
                h_m = inc(v.tensor_scalar(col(b + 3), col(b + 3), 0.125, None,
                                          ALU.mult), "selfV")
                wt(v, h_m)
                h_m2 = inc(v.tensor_tensor(col(b + 2), col(b + 3), col(b + 3),
                                           ALU.mult), "selfV")
                wt(v, h_m2)
                wt(v, h_ex)
                inc(v.tensor_tensor(col(b + 4), col(b + 4), col(b + 2),
                                    ALU.subtract), "aVv")

            inc(v.memset(col(80), -12.0), "aC12")
            _bEvac(v, 0, "bEvV", _p1unit)

        @blk.tensor
        def _(tn):
            def _bHalf(e, nb, bank):
                tn.wait_ge(S_(f"bWq{e % 3}"), 16 * (e // 3 + 1))
                for c in range(NC):
                    wch = wqb[e % 3][:, 128 * c:128 * (c + 1)]
                    mm = tn.matmul(pF[bank][:, :], wch,
                                   aT[:, 1024 * c + 512 * nb:
                                      1024 * c + 512 * (nb + 1)],
                                   start=(c == 0), stop=(c == NC - 1))
                return mm

            tn.wait_ge(S_("aCn"), 32)
            for t in range(NT):
                tn.wait_ge(S_("aVn"), t + 1)
                for bb in range(4):
                    bq = 4 * t + bb
                    if bq >= 4:
                        tn.wait_ge(S_("aEv"), bq - 3)
                    for j in range(4):
                        c = 4 * bb + j
                        tr = tn.transpose(pH[bq % 4][:, 128 * j:128 * (j + 1)],
                                          abuf[t % 2][:, 128 * c:128 * (c + 1)],
                                          ident[:, :])
                    inc(tr, "aTt")
                if t == 5:
                    tn.wait_ge(S_("aEv"), 16)
                    _bHalf(0, 0, 0)
                    _bHalf(1, 0, 2)
            tn.wait_ge(S_("aEv"), 32)
            for e in range(2):
                mm = _bHalf(e, 1, 2 * e + 1)
                inc(mm, "bMM")
            for e in range(2, NE):
                tn.wait_ge(S_(f"bWq{e % 3}"), 16 * (e // 3 + 1))
                tn.wait_ge(S_("bEvV"), e - 1)
                tn.wait_ge(S_("bEvS"), e - 1)
                pa, pb = pF[2 * (e % 2)], pF[2 * (e % 2) + 1]
                for c in range(NC):
                    wch = wqb[e % 3][:, 128 * c:128 * (c + 1)]
                    st, sp = (c == 0), (c == NC - 1)
                    tn.matmul(pa[:, :], wch, aT[:, 1024 * c:1024 * c + 512],
                              start=st, stop=sp)
                    mm = tn.matmul(pb[:, :], wch,
                                   aT[:, 1024 * c + 512:1024 * c + 1024],
                                   start=st, stop=sp)
                inc(mm, "bMM")

        @blk.gpsimd
        def _(g):
            for e in range(NE):
                if e >= 3:
                    g.wait_ge(S_("bMM"), e - 2)
                src = d_wqkvT[:, 128 * e:128 * (e + 1)].rearrange(
                    "(c p) e -> p c e", p=128)
                dst = wqb[e % 3][:, :].rearrange("p (c e) -> p c e", c=NC)
                inc(g.dma_start(dst, src), f"bWq{e % 3}", 16)

    if PHASES >= 3:
     def slab(m, b):
        return 128 * (8 * m + b)

     with nc.Block() as blk:
        @blk.tensor
        def _(tn):
            for g in range(NG + 1):
                if g < NG:
                    m, cg = g // 2, g % 2
                    if g >= 2:
                        tn.wait_ge(S_("cV8e"), g - 1)
                    for j in range(4):
                        sl = slab(m, 4 * cg + j)
                        tr = tn.transpose(pH[2 + g % 2][:, 128 * j:128 * (j + 1)],
                                          vT[:, sl:sl + 128], ident[:, :])
                    inc(tr, "cV8t")
                    if g >= 2:
                        tn.wait_ge(S_("cExp"), g - 1)
                    for j in range(4):
                        sl = slab(m, 4 * cg + j)
                        mm = tn.matmul(pF[g % 2][:, 128 * j:128 * (j + 1)],
                                       qT[:, sl:sl + 128], kT[:, sl:sl + 128],
                                       start=True, stop=True)
                    inc(mm, "cA8")
                k = g - 1
                if k >= 0:
                    tn.wait_ge(S_("cAsc"), k + 1)
                    if k >= 2:
                        tn.wait_ge(S_("cT1e"), k - 1)
                    for j in range(4):
                        tr = tn.transpose(pH[k % 2][:, 128 * j:128 * (j + 1)],
                                          Abuf[k % 2][:, 128 * j:128 * (j + 1)],
                                          ident[:, :])
                    inc(tr, "cAtT")
                    tn.wait_ge(S_("cT1e"), k + 1)
                    tn.wait_ge(S_("cV8e"), k + 1)
                    if k >= 2:
                        tn.wait_ge(S_("cO8e"), k - 1)
                    for j in range(4):
                        mm = tn.matmul(pF[2 + k % 2][:, 128 * j:128 * (j + 1)],
                                       V8buf[k % 2][:, 128 * j:128 * (j + 1)],
                                       aTTbuf[k % 2][:, 128 * j:128 * (j + 1)],
                                       start=True, stop=True)
                    inc(mm, "cO8")

        @blk.scalar
        def _(s):
            s.wait_ge(S_("aC12"), 1)
            for g in range(NG + 1):
                if g < NG:
                    s.wait_ge(S_("cA8"), g + 1)
                    if g >= 2:
                        s.wait_ge(S_("cAsc"), g - 1)
                    inc(s.activation(Ebuf[g % 2][:, :], pF[g % 2][:, :], AF.Exp,
                                     bias=col(80), scale=SCALE), "cExp")
                    s.wait_ge(S_("cV8t"), g + 1)
                    if g >= 2:
                        s.wait_ge(S_("cO8"), g - 1)
                    inc(s.copy(V8buf[g % 2][:, :], pH[2 + g % 2][:, :]), "cV8e")
                k = g - 1
                if k >= 0:
                    s.wait_ge(S_("cAtT"), k + 1)
                    if k >= 2:
                        s.wait_ge(S_("cO8"), k - 1)
                    inc(s.copy(aTTbuf[k % 2][:, :], pH[k % 2][:, :]), "cT1e")
                    s.wait_ge(S_("cO8"), k + 1)
                    km, kc = k // 2, k % 2
                    dst = sT.rearrange("p (t x) -> p t x", t=8)[
                        :, 4 * kc:4 * kc + 4, 128 * km:128 * (km + 1)]
                    inc(s.copy(
                        dst, pF[2 + k % 2][:, :].rearrange("p (j l) -> p j l", j=4)),
                        "cO8e")

        @blk.vector
        def _(v):
            for g in range(NG + 1):
                if g < NG:
                    p = g % 2
                    v.wait_ge(S_("cExp"), g + 1)
                    h_mask = inc(v.tensor_tensor(Embuf[p][:, :], Ebuf[p][:, :],
                                                 mask4[:, :], ALU.mult), "selfV")
                    wt(v, h_mask)
                    h_rs = inc(v.reduce_sum(
                        C_S4(p), Embuf[p][:, :].rearrange("p (b n) -> p b n", b=4),
                        AX.X), "selfV")
                    wt(v, h_rs)
                    h_rc = inc(v.reciprocal(C_R4(p), C_S4(p)), "selfV")
                    if g >= 2:
                        v.wait_ge(S_("cAtT"), g - 1)
                    wt(v, h_rc)
                    for j in range(4):
                        instr = v.tensor_scalar(Abuf[p][:, 128 * j:128 * (j + 1)],
                                                Embuf[p][:, 128 * j:128 * (j + 1)],
                                                C_R4(p)[:, j:j + 1], None, ALU.mult)
                        if j < 3:
                            inc(instr, "selfV")
                    inc(instr, "cAsc")



        @blk.gpsimd
        def _(g):
            for m in range(NC):
                g.wait_ge(S_("cV8t"), 2 * m + 2)
                if m >= 4:
                    g.wait_ge(S_(f"dWp{m % 4}"), 16 * (m // 4))
                src = d_wpT[128 * m:128 * (m + 1), 0:1024]
                inc(g.dma_start(wpb[:, 1024 * m:1024 * (m + 1)], src),
                    f"dWp{m % 4}", 16)
            for m in range(NC):
                g.wait_ge(S_("dMM"), 8)
                if m >= 4:
                    g.wait_ge(S_(f"dWp{m % 4}"), 64 + 16 * (m // 4))
                src = d_wpT[128 * m:128 * (m + 1), 1024:2048]
                inc(g.dma_start(wpb[:, 1024 * m:1024 * (m + 1)], src),
                    f"dWp{m % 4}", 16)
            for k in range(NT - 1):
                g.wait_ge(S_("gV1"), k + 1)
                inc(g.tensor_tensor(xr[:, 2048 * k:2048 * (k + 1)],
                                    xr[:, 2048 * k:2048 * (k + 1)],
                                    junkG2[:, :], ALU.add), "gOutR")
            for hh2 in range(2):
                g.wait_ge(S_("gV1h"), hh2 + 1)
                c0 = 2048 * (NT - 1) + 1024 * hh2
                inc(g.tensor_tensor(xr[:, c0:c0 + 1024], xr[:, c0:c0 + 1024],
                                    junkG2[:, 1024 * hh2:1024 * (hh2 + 1)],
                                    ALU.add), "gOutRh")

        @blk.sync
        def _(sync):
            _xq = {2: 0, 3: 0, 4: 1, 5: 1, 6: 2, 7: 2, 0: 3, 1: 3}
            for t in (2, 3, 4, 5, 6, 7, 0, 1):
                if t < 2:
                    sync.wait_ge(S_("cO8e"), NG)
                if _xq[t] >= 1:
                    sync.wait_ge(S_(f"dXr{t % 2}"), 16 * _xq[t])
                inc(sync.dma_start(xr[:, 2048 * t:2048 * (t + 1)],
                                   d_xres[128 * t:128 * (t + 1), :]),
                    f"dXr{t % 2}", 16)
            for idx in range(2 * NC):
                ft, hh = idx // 2, idx % 2
                sync.wait_ge(S_("fK1"), 2 * NC)
                if idx >= 2:
                    sync.wait_ge(S_("fKv"), idx - 1)
                    sync.wait_ge(S_("fKg"), idx - 1)
                inc(sync.dma_start(ktb[idx % 2][:, :],
                                   d_wkT[128 * ft:128 * (ft + 1),
                                         1024 * hh:1024 * (hh + 1)]),
                    f"fKt2{idx % 2}", 16)
            sync.wait_ge(S_("dMM"), 16)
            inc(sync.dma_start(bias_b[:, :], d_bias.partition_broadcast(128)),
                "gBb", 16)
            for Mt in range(NT - 1):
                sync.wait_ge(S_("gOutR"), Mt + 1)
                inc(sync.dma_start(d_out[128 * Mt:128 * (Mt + 1), :],
                                   xr[:, 2048 * Mt:2048 * (Mt + 1)]), "gOut", 16)
            for hh2 in range(2):
                sync.wait_ge(S_("gOutRh"), hh2 + 1)
                c0 = 2048 * (NT - 1) + 1024 * hh2
                inc(sync.dma_start(
                    d_out[128 * (NT - 1):128 * NT, 1024 * hh2:1024 * (hh2 + 1)],
                    xr[:, c0:c0 + 1024]), "gOut", 16)
            sync.wait_ge(S_("gOut"), 16 * (NT + 1))

        @blk.tensor
        def _(tn):
            def _fT(fidx):
                tn.wait_ge(S_("fKg"), fidx + 1)
                for b2 in range(2):
                    slot = (2 * fidx + b2) % 4
                    if 2 * fidx + b2 >= 4:
                        tn.wait_ge(S_("fFe"), 2 * fidx + b2 - 3)
                    for j in range(4):
                        cc = 4 * b2 + j
                        tr = tn.transpose(
                            pH[slot][:, 128 * j:128 * (j + 1)],
                            kqb[fidx % 3][:, 128 * cc:128 * (cc + 1)],
                            ident[:, :])
                    inc(tr, "fFt")

            tn.wait_ge(S_("cO8e"), NG)
            for h in range(2):
                for Mt in range(NT):
                    idx = 8 * h + Mt
                    if idx >= 2:
                        tn.wait_ge(S_("dEv"), 2 * idx - 2)
                    if Mt == 1:
                        for ln in range(4):
                            tn.wait_ge(S_(f"dWp{ln}"), 64 * (h + 1))
                    for m in range(NC):
                        if Mt == 0:
                            tn.wait_ge(S_(f"dWp{m % 4}"),
                                       64 * h + 16 * (m // 4 + 1))
                        st, sp = (m == 0), (m == NC - 1)
                        lhs = sT[:, 2048 * Mt + 128 * m:2048 * Mt + 128 * (m + 1)]
                        tn.matmul(pF[2 * (idx % 2)][:, :], lhs,
                                  wpb[:, 1024 * m:1024 * m + 512],
                                  start=st, stop=sp)
                        mm = tn.matmul(pF[2 * (idx % 2) + 1][:, :], lhs,
                                       wpb[:, 1024 * m + 512:1024 * (m + 1)],
                                       start=st, stop=sp)
                    inc(mm, "dMM")
                    for fidx in (2 * idx, 2 * idx + 1):
                        _fT(fidx)
            def _gTp(Mt):
                tn.wait_ge(S_("gGx"), Mt + 1)
                if Mt == 0:
                    tn.wait_ge(S_("fFe"), 64)
                else:
                    tn.wait_ge(S_("gXq"), 4 * Mt)
                for bb in range(4):
                    for j in range(4):
                        c = 4 * bb + j
                        tr = tn.transpose(pH[bb][:, 128 * j:128 * (j + 1)],
                                          qx[:, 128 * c:128 * (c + 1)],
                                          ident[:, :])
                    inc(tr, "gTp")

            for Mt in range(NT):
                if Mt == 0:
                    _gTp(0)
                    tn.wait_ge(S_("dEv"), 32)
                for u in range(4):
                    if Mt >= 1:
                        if u < 2:
                            tn.wait_ge(S_("gP1"), 2 * (Mt - 1) + u + 1)
                        else:
                            tn.wait_ge(S_("gP1V"), 2 * (Mt - 1) + u - 1)
                    for c in range(NC):
                        if u == 0 and c % 4 == 0:
                            tn.wait_ge(S_("gXq"), 4 * Mt + c // 4 + 1)
                        mm = tn.matmul(
                            pF[u][:, :],
                            xqs[Mt % 2][:, 128 * c:128 * (c + 1)],
                            qk[:, 2048 * c + 512 * u:2048 * c + 512 * (u + 1)],
                            start=(c == 0), stop=(c == NC - 1))
                    inc(mm, "gMM")
                    if u == 1 and Mt + 1 < NT:
                        _gTp(Mt + 1)

        hs0 = {}

        def _dEvac(v, idx):
            h, Mt = idx // 8, idx % 8
            v.wait_ge(S_("dMM"), idx + 1)
            _xpos = {2: 1, 4: 2, 6: 3, 0: 4, 3: 1, 5: 2, 7: 3, 1: 4}[Mt]
            v.wait_ge(S_(f"dXr{Mt % 2}"), 16 * _xpos)
            h_ev = None
            for uu in range(2):
                c0 = 2048 * Mt + 512 * (2 * h + uu)
                h_ev = inc(v.tensor_tensor(xr[:, c0:c0 + 512], xr[:, c0:c0 + 512],
                                           pF[2 * (idx % 2) + uu][:, :], ALU.add),
                           "dEv")
            if h == 0:
                xh = xr[:, 2048 * Mt:2048 * Mt + 1024]
                wt(v, h_ev)
                h_mx0 = inc(v.reduce_max(cols[:, 32 + Mt:33 + Mt], xh, AX.X),
                            "selfV")
                h_mn0 = inc(v.tensor_reduce(mn0buf[:, Mt:Mt + 1], xh, AX.X,
                                            ALU.min), "selfV")
                hs0[Mt] = (h_mx0, h_mn0)

        def _eStats1(v, Mt):
            b = 16 * (Mt % 2)
            xrt = xr[:, 2048 * Mt:2048 * (Mt + 1)]
            v.wait_ge(S_("dEv"), 18 + 2 * Mt)
            if Mt >= 2:
                v.wait_ge(S_("gSr"), Mt - 1)
            h_x1 = inc(v.reduce_max(col(b + 5), xrt[:, 1024:2048], AX.X),
                       "selfV")
            h_n1 = inc(v.tensor_reduce(col(b + 6), xrt[:, 1024:2048], AX.X,
                                       ALU.min), "selfV")
            wt(v, hs0[Mt][0])
            wt(v, h_x1)
            h_mx = inc(v.tensor_tensor(cols[:, 48 + Mt:49 + Mt],
                                       cols[:, 32 + Mt:33 + Mt], col(b + 5),
                                       ALU.max), "selfV")
            wt(v, hs0[Mt][1])
            wt(v, h_n1)
            h_mn = inc(v.tensor_tensor(cols[:, 56 + Mt:57 + Mt],
                                       mn0buf[:, Mt:Mt + 1], col(b + 6),
                                       ALU.min), "selfV")
            h_bn = None
            for q4 in range(4):
                h_bn = inc(v.bn_stats(bnbuf[:, 6 * q4:6 * q4 + 6],
                                      xrt[:, 512 * q4:512 * (q4 + 1)]), "selfV")
            wt(v, h_bn)
            mv = bnbuf[:, 0:24].rearrange("p (g c) -> p c g", g=8, c=3)
            h_sm = inc(v.reduce_sum(cols[:, 32 + Mt:33 + Mt], mv[:, 1, :], AX.X),
                       "selfV")
            h_sq = inc(v.tensor_tensor(cols[:, 80:88], mv[:, 1, :], mv[:, 1, :],
                                       ALU.mult), "selfV")
            h_sv = inc(v.reduce_sum(col(b + 2), mv[:, 2, :], AX.X), "selfV")
            wt(v, h_sq)
            h_s2 = inc(v.reduce_sum(col(b + 4), cols[:, 80:88], AX.X), "selfV")
            wt(v, h_sv)
            h_cv = inc(v.tensor_scalar(col(b + 2), col(b + 2), 1.0 / 2048, EPS,
                                       ALU.mult, ALU.add), "selfV")
            wt(v, h_s2)
            wt(v, h_cv)
            h_ex = inc(v.scalar_tensor_tensor(col(b + 4), col(b + 4), 0.125,
                                              col(b + 2), ALU.mult, ALU.add),
                       "selfV")
            wt(v, h_sm)
            h_m = inc(v.tensor_scalar(cols[:, 32 + Mt:33 + Mt],
                                      cols[:, 32 + Mt:33 + Mt], 0.125, None,
                                      ALU.mult), "selfV")
            wt(v, h_m)
            h_m2 = inc(v.tensor_tensor(col(b + 2), cols[:, 32 + Mt:33 + Mt],
                                       cols[:, 32 + Mt:33 + Mt], ALU.mult),
                       "selfV")
            wt(v, h_m2)
            wt(v, h_ex)
            inc(v.tensor_tensor(col(b + 4), col(b + 4), col(b + 2),
                                ALU.subtract), "gVv")
            return (h_mx, h_mn)

        @blk.vector
        def _(v):
            hs = {}
            hmag = {}

            def _magic(t):
                v.wait_ge(S_("gU2"), t + 1)
                hmag[t] = inc(v.tensor_scalar_add(xu2[:, :], xu2[:, :], MAGIC),
                              "selfV")

            def _gXiC(t):
                if t >= 1:
                    v.wait_ge(S_("gGx"), t)
                else:
                    v.wait_ge(S_("dMM"), 12)
                wt(v, hmag[t])
                inc(v.tensor_scalar(xi32[:, :], xu2[:, :], MAGIC, None,
                                    ALU.subtract), "gXi")

            def _part2(t):
                b = 16 * (t % 2)
                v.wait_ge(S_("gSr"), t + 1)
                wt(v, hs[t][0])
                wt(v, hs[t][1])
                h_a = inc(v.tensor_tensor(col(b + 7), cols[:, 48 + t:49 + t],
                                          cols[:, 32 + t:33 + t],
                                          ALU.subtract), "selfV")
                h_b = inc(v.tensor_tensor(col(b + 8), cols[:, 32 + t:33 + t],
                                          cols[:, 56 + t:57 + t],
                                          ALU.subtract), "selfV")
                wt(v, h_a)
                wt(v, h_b)
                h_c = inc(v.tensor_tensor(col(b + 9), col(b + 7), col(b + 8),
                                          ALU.max), "selfV")
                wt(v, h_c)
                h_d = inc(v.tensor_tensor(col(b + 9), col(b + 9),
                                          cols[:, 40 + t:41 + t],
                                          ALU.mult), "selfV")
                wt(v, h_d)
                h_e = inc(v.tensor_scalar(col(b + 10), col(b + 9), Q, GUARD,
                                          ALU.mult, ALU.max), "selfV")
                wt(v, h_e)
                h_f = inc(v.reciprocal(col(b + 11), col(b + 10)), "selfV")
                h_i = inc(v.tensor_copy(cols[:, 48 + t:49 + t], col(b + 10)),
                          "selfV")
                wt(v, h_f)
                h_g = inc(v.tensor_tensor(cols[:, 40 + t:41 + t],
                                          cols[:, 40 + t:41 + t],
                                          col(b + 11), ALU.mult), "selfV")
                wt(v, h_g)
                wt(v, h_i)
                inc(v.scalar_tensor_tensor(cols[:, 32 + t:33 + t],
                                           cols[:, 32 + t:33 + t], -1.0,
                                           cols[:, 40 + t:41 + t], ALU.mult,
                                           ALU.mult), "gVb")

            for i in range(2 * NC):
                ft = i // 2
                if i >= 4 and i % 2 == 0:
                    _dEvac(v, (i - 4) // 2)
                    if i >= 20:
                        hs[(i - 20) // 2] = _eStats1(v, (i - 20) // 2)
                v.wait_ge(S_(f"fKt2{i % 2}"), 16 * (i // 2 + 1))
                wt(v, h_sk[ft])
                h_qa = inc(v.tensor_scalar_mul(ktb[i % 2][:, :], ktb[i % 2][:, :],
                                               cols[:, 64 + ft:65 + ft]), "selfV")
                wt(v, h_qa)
                h_qb = inc(v.tensor_scalar_add(ktb[i % 2][:, :], ktb[i % 2][:, :],
                                               MAGIC), "selfV")
                wt(v, h_qb)
                h_kv = inc(v.tensor_scalar(kti[i % 2][:, :], ktb[i % 2][:, :],
                                           MAGIC, None, ALU.subtract), "fKv")
                if i >= 3:
                    v.wait_ge(S_("fFt"), 2 * i - 4)
                wt(v, h_kv)
                inc(v.tensor_scalar_mul(kqb[i % 3][:, :], kti[i % 2][:, :],
                                        cols[:, 48 + ft:49 + ft]), "fKg")
                if i == 24:
                    _part2(0)
                elif i == 26:
                    _part2(1)
            for j in range(14, 16):
                _dEvac(v, j)

            for Mt in range(NT + 2):
                if Mt < NT:
                    if Mt >= 1:
                        if Mt + 1 < NT:
                            _part2(Mt + 1)
                        _magic(Mt)
                        _gXiC(Mt)
                    if Mt == 0:
                        _magic(0)
                        _gXiC(0)
                        hs[6] = _eStats1(v, 6)
                        hs[7] = _eStats1(v, 7)
                        v.wait_ge(S_("dMM"), 16)
                        v.wait_ge(S_("gBb"), 16)
                        h1 = inc(v.reduce_max(col(81), bias_b[:, :], AX.X,
                                              apply_absolute_value=True), "selfV")
                        wt(v, h1)
                        h2 = inc(v.tensor_scalar(col(82), col(81), Q, GUARD, ALU.mult,
                                                 ALU.max), "selfV")
                        wt(v, h2)
                        h3 = inc(v.reciprocal(col(83), col(82)), "selfV")
                        wt(v, h3)
                        h4 = inc(v.tensor_scalar(junkG2[:, :], bias_b[:, :], col(83), MAGIC,
                                                 ALU.mult, ALU.add), "selfV")
                        wt(v, h4)
                        h5 = inc(v.tensor_scalar(bqi[:, :], junkG2[:, :], MAGIC, None,
                                                 ALU.subtract), "selfV")
                        wt(v, h5)
                        inc(v.tensor_scalar(bqb[:, :], bqi[:, :], col(82), None, ALU.mult),
                            "gBq")

                k = Mt - 2
                if k == NT - 1:
                    b = 16 * (k % 2)
                    v.wait_ge(S_("gP1"), 2 * k + 2)
                    h_y0 = inc(v.tensor_tensor(ttile[:, 0:1024], ttile[:, 0:1024],
                                               bqb[:, 0:1024], ALU.add), "selfV")
                    wt(v, h_y0)
                    h_y0m = inc(v.reduce_max(col(b + 13), ttile[:, 0:1024],
                                             AX.X), "selfV")
                    for u in range(2, 4):
                        v.wait_ge(S_("gMM"), 4 * k + u + 1)
                        inc(v.tensor_scalar(ttile[:, 512 * u:512 * (u + 1)],
                                            pF[u][:, :], cols[:, 48 + k:49 + k],
                                            None, ALU.mult), "gP1V")
                    v.wait_ge(S_("gP1V"), 2 * k + 2)
                    h_y1 = inc(v.tensor_tensor(ttile[:, 1024:2048],
                                               ttile[:, 1024:2048],
                                               bqb[:, 1024:2048], ALU.add),
                               "selfV")
                    wt(v, h_y1)
                    h_y1m = inc(v.reduce_max(col(b + 14), ttile[:, 1024:2048],
                                             AX.X), "selfV")
                    wt(v, h_y0m)
                    wt(v, h_y1m)
                    h_ym = inc(v.tensor_tensor(col(b + 13), col(b + 13),
                                               col(b + 14), ALU.max), "selfV")
                    wt(v, h_ym)
                    h_c0 = inc(v.tensor_single_scalar(col(b + 13), col(b + 13),
                                                      0.0, ALU.max), "selfV")
                    wt(v, h_c0)
                    h_sy = inc(v.tensor_scalar(cols[:, 56 + k:57 + k], col(b + 13),
                                               Q, GUARD, ALU.mult, ALU.max),
                               "selfV")
                    wt(v, h_sy)
                    h_ry = inc(v.reciprocal(cols[:, 64 + k:65 + k],
                                            cols[:, 56 + k:57 + k]), "selfV")
                    v.wait_ge(S_("gOutR"), k)
                    wt(v, h_ry)
                    for hh2 in range(2):
                        cs = slice(1024 * hh2, 1024 * (hh2 + 1))
                        h_z = inc(v.tensor_scalar(junkG2[:, cs], ttile[:, cs],
                                                  cols[:, 64 + k:65 + k],
                                                  MAGIC, ALU.mult, ALU.add),
                                  "gVz")
                        wt(v, h_z)
                        h_yi = inc(v.tensor_scalar(itile[:, cs],
                                                   junkG2[:, cs], MAGIC,
                                                   None, ALU.subtract),
                                   "selfV")
                        wt(v, h_yi)
                        inc(v.tensor_scalar(junkG2[:, cs], itile[:, cs],
                                            cols[:, 56 + k:57 + k], 0.0,
                                            ALU.mult, ALU.max), "gV1h")
                elif k >= 0:
                    b = 16 * (k % 2)
                    for u in range(2, 4):
                        v.wait_ge(S_("gMM"), 4 * k + u + 1)
                        inc(v.tensor_scalar(ttile[:, 512 * u:512 * (u + 1)],
                                            pF[u][:, :], cols[:, 48 + k:49 + k],
                                            None, ALU.mult), "gP1V")
                    v.wait_ge(S_("gP1"), 2 * k + 2)
                    v.wait_ge(S_("gP1V"), 2 * k + 2)
                    h_y = inc(v.tensor_tensor(ttile[:, :], ttile[:, :],
                                              bqb[:, :], ALU.add), "selfV")
                    wt(v, h_y)
                    h_ym = inc(v.reduce_max(col(b + 13), ttile[:, :], AX.X),
                               "selfV")
                    wt(v, h_ym)
                    h_c0 = inc(v.tensor_single_scalar(col(b + 13), col(b + 13),
                                                      0.0, ALU.max), "selfV")
                    wt(v, h_c0)
                    h_sy = inc(v.tensor_scalar(cols[:, 56 + k:57 + k], col(b + 13),
                                               Q, GUARD, ALU.mult, ALU.max),
                               "selfV")
                    wt(v, h_sy)
                    h_ry = inc(v.reciprocal(cols[:, 64 + k:65 + k],
                                            cols[:, 56 + k:57 + k]), "selfV")
                    if k >= 1:
                        v.wait_ge(S_("gOutR"), k)
                    wt(v, h_ry)
                    h_z = inc(v.tensor_scalar(junkG2[:, :], ttile[:, :],
                                              cols[:, 64 + k:65 + k], MAGIC,
                                              ALU.mult, ALU.add), "gVz")
                    wt(v, h_z)
                    h_yi = inc(v.tensor_scalar(itile[:, :], junkG2[:, :],
                                               MAGIC, None, ALU.subtract),
                               "selfV")
                    wt(v, h_yi)
                    inc(v.tensor_scalar(junkG2[:, :], itile[:, :],
                                        cols[:, 56 + k:57 + k], 0.0,
                                        ALU.mult, ALU.max), "gV1")


        def _sLnexp(s, Mt):
            b = 16 * (Mt % 2)
            s.wait_ge(S_("gVv"), Mt + 1)
            h_ln = inc(s.activation(col(b + 2), col(b + 4), AF.Ln), "selfS")
            wt(s, h_ln)
            inc(s.activation(cols[:, 40 + Mt:41 + Mt], col(b + 2), AF.Exp,
                             scale=-0.5), "gSr")

        @blk.scalar
        def _(s):
            def _u2(t):
                s.wait_ge(S_("gVb"), t + 1)
                if t == 0:
                    s.wait_ge(S_("dMM"), 10)
                else:
                    s.wait_ge(S_("gXi"), t)
                inc(s.activation(xu2[:, :], xr[:, 2048 * t:2048 * (t + 1)],
                                 AF.Identity, bias=cols[:, 32 + t:33 + t],
                                 scale=cols[:, 40 + t:41 + t]), "gU2")

            def _gGx(t):
                s.wait_ge(S_("gXi"), t + 1)
                if t >= 1:
                    s.wait_ge(S_("gTp"), 4 * t)
                else:
                    s.wait_ge(S_("dMM"), 13)
                inc(s.copy(qx[:, :], xi32[:, :]), "gGx")

            for fidx in range(2 * NC):
                ft, hh = fidx // 2, fidx % 2
                for b2 in range(2):
                    slot = (2 * fidx + b2) % 4
                    s.wait_ge(S_("fFt"), 2 * fidx + b2 + 1)
                    dst = qk.rearrange("p (c f) -> p c f", c=NC)[
                        :, 8 * hh + 4 * b2:8 * hh + 4 * b2 + 4,
                        128 * ft:128 * (ft + 1)]
                    src = pH[slot][:, :].rearrange("p (c f) -> p c f", c=4)
                    inc(ecopy(s, dst, src), "fFe")
                if fidx == 28:
                    _u2(0)
                if fidx % 2 == 0 and 18 <= fidx <= 28:
                    _sLnexp(s, (fidx - 18) // 2)
            for Mt in range(NT + 1):
                if Mt < NT:
                    _gGx(Mt)
                    if Mt + 1 < NT:
                        _u2(Mt + 1)
                    if Mt in (0, 1):
                        _sLnexp(s, 6 + Mt)
                    if Mt >= 2:
                        s.wait_ge(S_("gMM"), 4 * Mt - 4)
                    for bb in range(4):
                        s.wait_ge(S_("gTp"), 4 * Mt + bb + 1)
                        if Mt == 0 and bb == 0:
                            s.wait_ge(S_("dMM"), 16)
                        inc(s.copy(xqs[Mt % 2][:, 512 * bb:512 * (bb + 1)],
                                   pH[bb][:, :]), "gXq")
                k = Mt - 1
                if k >= 0:
                    if k == 0:
                        s.wait_ge(S_("gBq"), 1)
                    else:
                        s.wait_ge(S_("gVz"), k)
                    for u in range(2):
                        s.wait_ge(S_("gMM"), 4 * k + u + 1)
                        inc(s.activation(ttile[:, 512 * u:512 * (u + 1)],
                                         pF[u][:, :], AF.Copy,
                                         scale=cols[:, 48 + k:49 + k]), "gP1")

    if PHASES < 4:
        with nc.Block() as blk:
            @blk.sync
            def _(sync):
                for Mt in range(NT):
                    inc(sync.dma_start(d_out[128 * Mt:128 * (Mt + 1), :],
                                       xr[:, 2048 * Mt:2048 * (Mt + 1)]),
                        "zOut", 16)
                sync.wait_ge(S_("zOut"), 16 * NT)
    return nc


_CACHE = {}


def _host_inputs(L0_attn_x, L0_attn_w_qkv, L0_attn_w_proj, L0_ffn_kernel,
                 L0_ffn_bias):
    x = np.ascontiguousarray(np.asarray(L0_attn_x, dtype=np.float32)[0])
    WqkvT = np.ascontiguousarray(np.asarray(L0_attn_w_qkv, dtype=np.float32).T)
    WpT = np.ascontiguousarray(np.asarray(L0_attn_w_proj, dtype=np.float32).T)
    WkT = np.ascontiguousarray(np.asarray(L0_ffn_kernel, dtype=np.float32).T)
    biasr = np.ascontiguousarray(
        np.asarray(L0_ffn_bias, dtype=np.float32).reshape(1, D))
    ident = np.eye(128, dtype=np.float32).astype(BF)
    m = np.zeros((128, 128), dtype=np.float32)
    for t in range(8):
        m[t * 16:(t + 1) * 16, t * 16:(t + 1) * 16] = 1.0
    mask4 = np.ascontiguousarray(np.tile(m, (1, 4))).astype(BF)
    g = np.arange(64)
    h = np.arange(16)
    in_maps, rows_all = [], []
    for k in range(8):
        rows = (h[None, :] * 512 + 64 * k + g[:, None]).reshape(-1)
        rows_all.append(rows)
        in_maps.append({
            "x_att": np.ascontiguousarray(x[1024 * k:1024 * (k + 1)]),
            "x_res": np.ascontiguousarray(x[rows]),
            "wqkvT": WqkvT, "wpT": WpT, "wkT": WkT, "bias": biasr,
            "ident": ident, "mask": mask4,
        })
    return in_maps, rows_all


def kernel(L0_attn_x, L0_attn_w_qkv, L0_attn_w_proj, L0_ffn_kernel, L0_ffn_bias):
    in_maps, rows_all = _host_inputs(L0_attn_x, L0_attn_w_qkv, L0_attn_w_proj,
                                     L0_ffn_kernel, L0_ffn_bias)
    key = PHASES
    if key not in _CACHE:
        _CACHE[key] = build_program()
    res = run_bass_kernel_spmd(_CACHE[key], in_maps, core_ids=list(range(8)))
    out = np.empty((1, 8192, 2048), dtype=np.float32)
    for k in range(8):
        out[0, rows_all[k]] = res.results[k]["out"]
    return out


# version 25
# speedup vs baseline: 1.0197x; 1.0197x over previous
import numpy as np
import ml_dtypes
import concourse.bass as bass
import concourse.mybir as mybir
from concourse.bass_utils import run_bass_kernel_spmd

BF = ml_dtypes.bfloat16
F32 = mybir.dt.float32
BF16 = mybir.dt.bfloat16
I32 = mybir.dt.int32
AF = mybir.ActivationFunctionType
ALU = mybir.AluOpType
AX = mybir.AxisListType

NL, D, H, HD, E = 1024, 2048, 16, 128, 6144
NT, NC, NE, NG = 8, 16, 48, 32
SCALE = float(HD) ** -0.5
EPS = 1e-5
GUARD = 1e-30
Q = 1.0 / 127.0
MAGIC = 12582912.0
INV_D = 1.0 / D

PHASES = 5


def build_program():
    nc = bass.Bass("TRN2", target_bir_lowering=False)

    d_xatt = nc.dram_tensor("x_att", [NL, D], F32, kind="ExternalInput").ap()
    d_xres = nc.dram_tensor("x_res", [NL, D], F32, kind="ExternalInput").ap()
    d_wqkvT = nc.dram_tensor("wqkvT", [D, E], F32, kind="ExternalInput").ap()
    d_wpT = nc.dram_tensor("wpT", [D, D], F32, kind="ExternalInput").ap()
    d_wkT = nc.dram_tensor("wkT", [D, D], F32, kind="ExternalInput").ap()
    d_bias = nc.dram_tensor("bias", [1, D], F32, kind="ExternalInput").ap()
    d_ident = nc.dram_tensor("ident", [128, 128], BF16, kind="ExternalInput").ap()
    d_mask = nc.dram_tensor("mask", [128, 512], BF16, kind="ExternalInput").ap()
    d_out = nc.dram_tensor("out", [NL, D], F32, kind="ExternalOutput").ap()

    BASE = 16640

    def A(name, shape, dt, offset):
        return nc.alloc_sbuf_tensor_at(name, shape, dt, offset=BASE + offset)

    aT = A("aT", [128, 16384], BF16, offset=0)
    xr = A("xr", [128, 16384], F32, offset=0)
    qT = A("qT", [128, 16384], BF16, offset=65536)
    kT = A("kT", [128, 16384], BF16, offset=98304)
    vT = A("vT", [128, 16384], BF16, offset=131072)
    xbuf = [A(f"xbuf{i}", [128, 2048], F32, offset=65536 + 8192 * i) for i in range(2)]
    abuf = [A(f"abuf{i}", [128, 2048], BF16, offset=81920 + 4096 * i) for i in range(2)]
    junkA = A("junkA", [128, 2048], F32, offset=90112)
    qk = A("qk", [128, 32768], BF16, offset=65536)
    wpb = A("wpb", [128, 16384], BF16, offset=131072)
    xqs = [A(f"xqs{i}", [128, 2048], BF16, offset=131072 + 4096 * i) for i in range(2)]
    itile = A("itile", [128, 2048], I32, offset=139264)
    bqi = A("bqi", [128, 2048], I32, offset=139264)
    junkG2 = A("junkG2", [128, 2048], F32, offset=147456)
    ttile = A("ttile", [128, 2048], F32, offset=155648)
    bias_b = A("bias_b", [128, 2048], F32, offset=155648)
    sT = A("sT", [128, 16384], BF16, offset=163840)
    xu2 = A("xu2", [128, 2048], F32, offset=163840)
    xi32 = A("xi32", [128, 2048], I32, offset=172032)
    qx = A("qx", [128, 2048], BF16, offset=180224)
    bqb = A("bqb", [128, 2048], F32, offset=184320)
    wqb = [A(f"wqb{i}", [128, 2048], BF16, offset=198240 + 4096 * i) for i in range(3)]
    Ebuf = [A(f"Ebuf{i}", [128, 512], BF16, offset=0 + 1024 * i) for i in range(2)]
    Embuf = [A(f"Embuf{i}", [128, 512], BF16, offset=2048 + 1024 * i) for i in range(2)]
    Abuf = [A(f"Abuf{i}", [128, 512], BF16, offset=4096 + 1024 * i) for i in range(2)]
    aTTbuf = [A(f"aTTbuf{i}", [128, 512], BF16, offset=6144 + 1024 * i) for i in range(2)]
    V8buf = [A(f"V8buf{i}", [128, 512], BF16, offset=8192 + 1024 * i) for i in range(2)]
    ktb = [A(f"ktb{i}", [128, 1024], F32, offset=198240 + 4096 * i) for i in range(2)]
    ktc = [A(f"ktc{i}", [128, 1024], F32, offset=32768 + 4096 * i) for i in range(2)]
    kti = [A(f"kti{i}", [128, 1024], I32, offset=198240 + 4096 * i) for i in range(2)]
    kqb = [A(f"kqb{i}", [128, 1024], BF16, offset=206432 + 2048 * i) for i in range(3)]
    ident = A("ident", [128, 128], BF16, offset=196608)
    mask4 = A("mask4", [128, 512], BF16, offset=196864)
    cols = A("cols", [128, 88], F32, offset=197888)
    bnbuf = A("bnbuf", [128, 24], F32, offset=212608)
    mn0buf = A("mn0buf", [128, 8], F32, offset=212576)

    def C_S4(p):
        return cols[:, 32 + 4 * p:36 + 4 * p]

    def C_R4(p):
        return cols[:, 40 + 4 * p:44 + 4 * p]

    pF = [nc.alloc_psum_tensor(f"pF{i}", [128, 512], F32) for i in range(4)]
    pH = [nc.alloc_psum_tensor(f"pH{i}", [128, 512], BF16) for i in range(4)]

    sems, cnt = {}, {}

    def S_(name):
        if name not in sems:
            sems[name] = nc.semaphore(name).__enter__()
            cnt[name] = 0
        return sems[name]

    def inc(instr, name, k=1):
        h = S_(name)
        instr.then_inc(h, k)
        cnt[name] += k
        return (name, cnt[name])

    def wt(eng, handle):
        eng.wait_ge(S_(handle[0]), handle[1])

    def col(i):
        return cols[:, i:i + 1]

    def ecopy(eng, dst, srcap):
        if hasattr(eng, "tensor_copy"):
            return eng.tensor_copy(dst, srcap)
        return eng.copy(dst, srcap)

    h_sk = {}
    with nc.Block() as blk:
        def _bEvac(eng, nb, semname, extra=None):
            for e in range(NE):
                if extra is not None and 8 <= e < 40:
                    extra(eng, e - 8)
                eng.wait_ge(S_("bMM"), e + 1)
                sec, head = e // 16, e % 16
                dstT = (qT, kT, vT)[sec]
                dst = dstT.rearrange("p (m b t i) -> p i b t m",
                                     m=16, b=8, t=8, i=16)[
                    :, head, 4 * nb:4 * nb + 4, :, :]
                src = pF[2 * (e % 2) + nb][:, :].rearrange(
                    "p (b t m) -> p b t m", b=4, t=8, m=16)
                inc(ecopy(eng, dst, src), semname)

        def _p1unit(v, i):
            ft, hh = i // 2, i % 2
            v.wait_ge(S_(f"fKt1{i % 2}"), 16 * (i // 2 + 1))
            h1 = inc(v.reduce_max(col(84 + hh), ktc[i % 2][:, :], AX.X,
                                  apply_absolute_value=True), "fK1")
            if hh == 1:
                wt(v, h1)
                h_cb = inc(v.tensor_tensor(col(84), col(84), col(85),
                                           ALU.max), "selfV")
                wt(v, h_cb)
                h_s = inc(v.tensor_scalar(cols[:, 48 + ft:49 + ft],
                                          col(84), Q, GUARD, ALU.mult,
                                          ALU.max), "selfV")
                wt(v, h_s)
                h_sk[ft] = inc(v.reciprocal(cols[:, 64 + ft:65 + ft],
                                            cols[:, 48 + ft:49 + ft]),
                               "selfV")

        @blk.sync
        def _(sync):
            inc(sync.dma_start(ident[:, :], d_ident[:, :]), "aCn", 16)
            inc(sync.dma_start(mask4[:, :], d_mask[:, :]), "aCn", 16)
            for t in range(NT):
                if t >= 2:
                    sync.wait_ge(S_("aVn"), t - 1)
                    sync.wait_ge(S_("aVv"), t - 1)
                inc(sync.dma_start(xbuf[t % 2][:, :],
                                   d_xatt[128 * t:128 * (t + 1), :]),
                    f"aLd{t % 2}", 16)
            for idx in range(2 * NC):
                ft, hh = idx // 2, idx % 2
                if idx >= 2:
                    sync.wait_ge(S_("fK1"), idx - 1)
                inc(sync.dma_start(ktc[idx % 2][:, :],
                                   d_wkT[128 * ft:128 * (ft + 1),
                                         1024 * hh:1024 * (hh + 1)]),
                    f"fKt1{idx % 2}", 16)

        def _aEvac(s, t):
            for bb in range(4):
                bq = 4 * t + bb
                s.wait_ge(S_("aTt"), bq + 1)
                dst = aT.rearrange("p (c n) -> p c n", c=NC)[
                    :, 4 * bb:4 * bb + 4, 128 * t:128 * (t + 1)]
                src = pH[bq % 4][:, :].rearrange("p (c n) -> p c n", c=4)
                inc(ecopy(s, dst, src), "aEv")

        @blk.scalar
        def _(s):
            for t in range(NT):
                b = 16 * (t % 2)
                s.wait_ge(S_("aVv"), t + 1)
                h_ln = inc(s.activation(col(b + 2), col(b + 4), AF.Ln), "selfS")
                wt(s, h_ln)
                h_rs = inc(s.activation(col(b + 5), col(b + 2), AF.Exp,
                                        scale=-0.5), "aSr")
                wt(s, h_rs)
                h_b1 = inc(s.mul(col(b + 6), col(b + 3), col(b + 5)), "selfS")
                wt(s, h_b1)
                h_b2 = inc(s.activation(col(b + 6), col(b + 6), AF.Identity,
                                        scale=-1.0), "selfS")
                wt(s, h_b2)
                if t >= 2:
                    s.wait_ge(S_("aTt"), 4 * t - 4)
                inc(s.activation(abuf[t % 2][:, :], xbuf[t % 2][:, :], AF.Identity,
                                 bias=col(b + 6), scale=col(b + 5)), "aVn")
                if t >= 1:
                    _aEvac(s, t - 1)
            _aEvac(s, NT - 1)
            _bEvac(s, 1, "bEvS")

        @blk.vector
        def _(v):
            for t in range(NT):
                v.wait_ge(S_(f"aLd{t % 2}"), 16 * (t // 2 + 1))
                b = 16 * (t % 2)
                xbt = xbuf[t % 2]
                h_bn = None
                for q4 in range(4):
                    h_bn = inc(v.bn_stats(bnbuf[:, 6 * q4:6 * q4 + 6],
                                          xbt[:, 512 * q4:512 * (q4 + 1)]),
                               "selfV")
                wt(v, h_bn)
                mv = bnbuf[:, 0:24].rearrange("p (g c) -> p c g", g=8, c=3)
                h_sm = inc(v.reduce_sum(col(b + 3), mv[:, 1, :], AX.X), "selfV")
                h_sq = inc(v.tensor_tensor(cols[:, 80:88], mv[:, 1, :],
                                           mv[:, 1, :], ALU.mult), "selfV")
                h_sv = inc(v.reduce_sum(col(b + 2), mv[:, 2, :], AX.X), "selfV")
                wt(v, h_sq)
                h_s2 = inc(v.reduce_sum(col(b + 4), cols[:, 80:88], AX.X),
                           "selfV")
                wt(v, h_sv)
                h_cv = inc(v.tensor_scalar(col(b + 2), col(b + 2), 1.0 / 2048,
                                           EPS, ALU.mult, ALU.add), "selfV")
                wt(v, h_s2)
                wt(v, h_cv)
                h_ex = inc(v.scalar_tensor_tensor(col(b + 4), col(b + 4), 0.125,
                                                  col(b + 2), ALU.mult, ALU.add),
                           "selfV")
                wt(v, h_sm)
                h_m = inc(v.tensor_scalar(col(b + 3), col(b + 3), 0.125, None,
                                          ALU.mult), "selfV")
                wt(v, h_m)
                h_m2 = inc(v.tensor_tensor(col(b + 2), col(b + 3), col(b + 3),
                                           ALU.mult), "selfV")
                wt(v, h_m2)
                wt(v, h_ex)
                inc(v.tensor_tensor(col(b + 4), col(b + 4), col(b + 2),
                                    ALU.subtract), "aVv")

            inc(v.memset(col(80), -12.0), "aC12")
            _bEvac(v, 0, "bEvV", _p1unit)

        @blk.tensor
        def _(tn):
            def _bHalf(e, nb, bank):
                tn.wait_ge(S_(f"bWq{e % 3}"), 16 * (e // 3 + 1))
                for c in range(NC):
                    wch = wqb[e % 3][:, 128 * c:128 * (c + 1)]
                    mm = tn.matmul(pF[bank][:, :], wch,
                                   aT[:, 1024 * c + 512 * nb:
                                      1024 * c + 512 * (nb + 1)],
                                   start=(c == 0), stop=(c == NC - 1))
                return mm

            tn.wait_ge(S_("aCn"), 32)
            for t in range(NT):
                tn.wait_ge(S_("aVn"), t + 1)
                for bb in range(4):
                    bq = 4 * t + bb
                    if bq >= 4:
                        tn.wait_ge(S_("aEv"), bq - 3)
                    for j in range(4):
                        c = 4 * bb + j
                        tr = tn.transpose(pH[bq % 4][:, 128 * j:128 * (j + 1)],
                                          abuf[t % 2][:, 128 * c:128 * (c + 1)],
                                          ident[:, :])
                    inc(tr, "aTt")
                if t == 5:
                    tn.wait_ge(S_("aEv"), 16)
                    _bHalf(0, 0, 0)
                    _bHalf(1, 0, 2)
            tn.wait_ge(S_("aEv"), 32)
            for e in range(2):
                mm = _bHalf(e, 1, 2 * e + 1)
                inc(mm, "bMM")
            for e in range(2, NE):
                tn.wait_ge(S_(f"bWq{e % 3}"), 16 * (e // 3 + 1))
                tn.wait_ge(S_("bEvV"), e - 1)
                tn.wait_ge(S_("bEvS"), e - 1)
                pa, pb = pF[2 * (e % 2)], pF[2 * (e % 2) + 1]
                for c in range(NC):
                    wch = wqb[e % 3][:, 128 * c:128 * (c + 1)]
                    st, sp = (c == 0), (c == NC - 1)
                    tn.matmul(pa[:, :], wch, aT[:, 1024 * c:1024 * c + 512],
                              start=st, stop=sp)
                    mm = tn.matmul(pb[:, :], wch,
                                   aT[:, 1024 * c + 512:1024 * c + 1024],
                                   start=st, stop=sp)
                inc(mm, "bMM")

        @blk.gpsimd
        def _(g):
            for e in range(NE):
                if e >= 3:
                    g.wait_ge(S_("bMM"), e - 2)
                src = d_wqkvT[:, 128 * e:128 * (e + 1)].rearrange(
                    "(c p) e -> p c e", p=128)
                dst = wqb[e % 3][:, :].rearrange("p (c e) -> p c e", c=NC)
                inc(g.dma_start(dst, src), f"bWq{e % 3}", 16)

    if PHASES >= 3:
     def slab(m, b):
        return 128 * (8 * m + b)

     with nc.Block() as blk:
        @blk.tensor
        def _(tn):
            for g in range(NG + 1):
                if g < NG:
                    m, cg = g // 2, g % 2
                    if g >= 2:
                        tn.wait_ge(S_("cV8e"), g - 1)
                    for j in range(4):
                        sl = slab(m, 4 * cg + j)
                        tr = tn.transpose(pH[2 + g % 2][:, 128 * j:128 * (j + 1)],
                                          vT[:, sl:sl + 128], ident[:, :])
                    inc(tr, "cV8t")
                    if g >= 2:
                        tn.wait_ge(S_("cExp"), g - 1)
                    for j in range(4):
                        sl = slab(m, 4 * cg + j)
                        mm = tn.matmul(pF[g % 2][:, 128 * j:128 * (j + 1)],
                                       qT[:, sl:sl + 128], kT[:, sl:sl + 128],
                                       start=True, stop=True)
                    inc(mm, "cA8")
                k = g - 1
                if k >= 0:
                    tn.wait_ge(S_("cAsc"), k + 1)
                    if k >= 2:
                        tn.wait_ge(S_("cT1e"), k - 1)
                    for j in range(4):
                        tr = tn.transpose(pH[k % 2][:, 128 * j:128 * (j + 1)],
                                          Abuf[k % 2][:, 128 * j:128 * (j + 1)],
                                          ident[:, :])
                    inc(tr, "cAtT")
                    tn.wait_ge(S_("cT1e"), k + 1)
                    tn.wait_ge(S_("cV8e"), k + 1)
                    if k >= 2:
                        tn.wait_ge(S_("cO8e"), k - 1)
                    for j in range(4):
                        mm = tn.matmul(pF[2 + k % 2][:, 128 * j:128 * (j + 1)],
                                       V8buf[k % 2][:, 128 * j:128 * (j + 1)],
                                       aTTbuf[k % 2][:, 128 * j:128 * (j + 1)],
                                       start=True, stop=True)
                    inc(mm, "cO8")

        @blk.scalar
        def _(s):
            s.wait_ge(S_("aC12"), 1)
            for g in range(NG + 1):
                if g < NG:
                    s.wait_ge(S_("cA8"), g + 1)
                    if g >= 2:
                        s.wait_ge(S_("cAsc"), g - 1)
                    inc(s.activation(Ebuf[g % 2][:, :], pF[g % 2][:, :], AF.Exp,
                                     bias=col(80), scale=SCALE), "cExp")
                    s.wait_ge(S_("cV8t"), g + 1)
                    if g >= 2:
                        s.wait_ge(S_("cO8"), g - 1)
                    inc(s.copy(V8buf[g % 2][:, :], pH[2 + g % 2][:, :]), "cV8e")
                k = g - 1
                if k >= 0:
                    s.wait_ge(S_("cAtT"), k + 1)
                    if k >= 2:
                        s.wait_ge(S_("cO8"), k - 1)
                    inc(s.copy(aTTbuf[k % 2][:, :], pH[k % 2][:, :]), "cT1e")
                    s.wait_ge(S_("cO8"), k + 1)
                    km, kc = k // 2, k % 2
                    dst = sT.rearrange("p (t x) -> p t x", t=8)[
                        :, 4 * kc:4 * kc + 4, 128 * km:128 * (km + 1)]
                    inc(s.copy(
                        dst, pF[2 + k % 2][:, :].rearrange("p (j l) -> p j l", j=4)),
                        "cO8e")

        @blk.vector
        def _(v):
            for g in range(NG + 1):
                if g < NG:
                    p = g % 2
                    v.wait_ge(S_("cExp"), g + 1)
                    h_mask = inc(v.tensor_tensor(Embuf[p][:, :], Ebuf[p][:, :],
                                                 mask4[:, :], ALU.mult), "selfV")
                    wt(v, h_mask)
                    h_rs = inc(v.reduce_sum(
                        C_S4(p), Embuf[p][:, :].rearrange("p (b n) -> p b n", b=4),
                        AX.X), "selfV")
                    wt(v, h_rs)
                    h_rc = inc(v.reciprocal(C_R4(p), C_S4(p)), "selfV")
                    if g >= 2:
                        v.wait_ge(S_("cAtT"), g - 1)
                    wt(v, h_rc)
                    for j in range(4):
                        instr = v.tensor_scalar(Abuf[p][:, 128 * j:128 * (j + 1)],
                                                Embuf[p][:, 128 * j:128 * (j + 1)],
                                                C_R4(p)[:, j:j + 1], None, ALU.mult)
                        if j < 3:
                            inc(instr, "selfV")
                    inc(instr, "cAsc")



        @blk.gpsimd
        def _(g):
            for m in range(NC):
                g.wait_ge(S_("cV8t"), 2 * m + 2)
                if m >= 4:
                    g.wait_ge(S_(f"dWp{m % 4}"), 16 * (m // 4))
                src = d_wpT[128 * m:128 * (m + 1), 0:1024]
                inc(g.dma_start(wpb[:, 1024 * m:1024 * (m + 1)], src),
                    f"dWp{m % 4}", 16)
            for m in range(NC):
                g.wait_ge(S_("dMM"), 8)
                if m >= 4:
                    g.wait_ge(S_(f"dWp{m % 4}"), 64 + 16 * (m // 4))
                src = d_wpT[128 * m:128 * (m + 1), 1024:2048]
                inc(g.dma_start(wpb[:, 1024 * m:1024 * (m + 1)], src),
                    f"dWp{m % 4}", 16)
            for k in range(NT - 1):
                g.wait_ge(S_("gV1"), k + 1)
                inc(g.tensor_tensor(xr[:, 2048 * k:2048 * (k + 1)],
                                    xr[:, 2048 * k:2048 * (k + 1)],
                                    junkG2[:, :], ALU.add), "gOutR")
            for hh2 in range(2):
                g.wait_ge(S_("gV1h"), hh2 + 1)
                c0 = 2048 * (NT - 1) + 1024 * hh2
                inc(g.tensor_tensor(xr[:, c0:c0 + 1024], xr[:, c0:c0 + 1024],
                                    junkG2[:, 1024 * hh2:1024 * (hh2 + 1)],
                                    ALU.add), "gOutRh")

        @blk.sync
        def _(sync):
            _xq = {2: 0, 3: 0, 4: 1, 5: 1, 6: 2, 7: 2, 0: 3, 1: 3}
            for t in (2, 3, 4, 5, 6, 7, 0, 1):
                if t < 2:
                    sync.wait_ge(S_("cO8e"), NG)
                if _xq[t] >= 1:
                    sync.wait_ge(S_(f"dXr{t % 2}"), 16 * _xq[t])
                inc(sync.dma_start(xr[:, 2048 * t:2048 * (t + 1)],
                                   d_xres[128 * t:128 * (t + 1), :]),
                    f"dXr{t % 2}", 16)
            for idx in range(2 * NC):
                ft, hh = idx // 2, idx % 2
                sync.wait_ge(S_("fK1"), 2 * NC)
                if idx >= 2:
                    sync.wait_ge(S_("fKv"), idx - 1)
                    sync.wait_ge(S_("fKg"), idx - 1)
                inc(sync.dma_start(ktb[idx % 2][:, :],
                                   d_wkT[128 * ft:128 * (ft + 1),
                                         1024 * hh:1024 * (hh + 1)]),
                    f"fKt2{idx % 2}", 16)
            sync.wait_ge(S_("dMM"), 16)
            inc(sync.dma_start(bias_b[:, :], d_bias.partition_broadcast(128)),
                "gBb", 16)
            for Mt in range(NT - 1):
                sync.wait_ge(S_("gOutR"), Mt + 1)
                inc(sync.dma_start(d_out[128 * Mt:128 * (Mt + 1), :],
                                   xr[:, 2048 * Mt:2048 * (Mt + 1)]), "gOut", 16)
            for hh2 in range(2):
                sync.wait_ge(S_("gOutRh"), hh2 + 1)
                c0 = 2048 * (NT - 1) + 1024 * hh2
                inc(sync.dma_start(
                    d_out[128 * (NT - 1):128 * NT, 1024 * hh2:1024 * (hh2 + 1)],
                    xr[:, c0:c0 + 1024]), "gOut", 16)
            sync.wait_ge(S_("gOut"), 16 * (NT + 1))

        @blk.tensor
        def _(tn):
            def _fT(fidx):
                tn.wait_ge(S_("fKg"), fidx + 1)
                for b2 in range(2):
                    slot = (2 * fidx + b2) % 4
                    if 2 * fidx + b2 >= 4:
                        tn.wait_ge(S_("fFe"), 2 * fidx + b2 - 3)
                    for j in range(4):
                        cc = 4 * b2 + j
                        tr = tn.transpose(
                            pH[slot][:, 128 * j:128 * (j + 1)],
                            kqb[fidx % 3][:, 128 * cc:128 * (cc + 1)],
                            ident[:, :])
                    inc(tr, "fFt")

            tn.wait_ge(S_("cO8e"), NG)
            for h in range(2):
                for Mt in range(NT):
                    idx = 8 * h + Mt
                    if idx >= 2:
                        tn.wait_ge(S_("dEv"), 2 * idx - 2)
                    if Mt == 1:
                        for ln in range(4):
                            tn.wait_ge(S_(f"dWp{ln}"), 64 * (h + 1))
                    for m in range(NC):
                        if Mt == 0:
                            tn.wait_ge(S_(f"dWp{m % 4}"),
                                       64 * h + 16 * (m // 4 + 1))
                        st, sp = (m == 0), (m == NC - 1)
                        lhs = sT[:, 2048 * Mt + 128 * m:2048 * Mt + 128 * (m + 1)]
                        tn.matmul(pF[2 * (idx % 2)][:, :], lhs,
                                  wpb[:, 1024 * m:1024 * m + 512],
                                  start=st, stop=sp)
                        mm = tn.matmul(pF[2 * (idx % 2) + 1][:, :], lhs,
                                       wpb[:, 1024 * m + 512:1024 * (m + 1)],
                                       start=st, stop=sp)
                    inc(mm, "dMM")
                    for fidx in (2 * idx, 2 * idx + 1):
                        _fT(fidx)
            def _gTp(Mt):
                tn.wait_ge(S_("gGx"), Mt + 1)
                if Mt == 0:
                    tn.wait_ge(S_("fFe"), 64)
                else:
                    tn.wait_ge(S_("gXq"), 4 * Mt)
                for bb in range(4):
                    for j in range(4):
                        c = 4 * bb + j
                        tr = tn.transpose(pH[bb][:, 128 * j:128 * (j + 1)],
                                          qx[:, 128 * c:128 * (c + 1)],
                                          ident[:, :])
                    inc(tr, "gTp")

            for Mt in range(NT):
                if Mt == 0:
                    _gTp(0)
                    tn.wait_ge(S_("dEv"), 32)
                for u in range(4):
                    if Mt >= 1:
                        if u < 2:
                            tn.wait_ge(S_("gP1"), 2 * (Mt - 1) + u + 1)
                        else:
                            tn.wait_ge(S_("gP1V"), 2 * (Mt - 1) + u - 1)
                    for c in range(NC):
                        if u == 0 and c % 4 == 0:
                            tn.wait_ge(S_("gXq"), 4 * Mt + c // 4 + 1)
                        mm = tn.matmul(
                            pF[u][:, :],
                            xqs[Mt % 2][:, 128 * c:128 * (c + 1)],
                            qk[:, 2048 * c + 512 * u:2048 * c + 512 * (u + 1)],
                            start=(c == 0), stop=(c == NC - 1))
                    inc(mm, "gMM")
                    if u == 1 and Mt + 1 < NT:
                        _gTp(Mt + 1)

        hs0 = {}

        def _dEvac(v, idx):
            h, Mt = idx // 8, idx % 8
            v.wait_ge(S_("dMM"), idx + 1)
            _xpos = {2: 1, 4: 2, 6: 3, 0: 4, 3: 1, 5: 2, 7: 3, 1: 4}[Mt]
            v.wait_ge(S_(f"dXr{Mt % 2}"), 16 * _xpos)
            h_ev = None
            for uu in range(2):
                c0 = 2048 * Mt + 512 * (2 * h + uu)
                h_ev = inc(v.tensor_tensor(xr[:, c0:c0 + 512], xr[:, c0:c0 + 512],
                                           pF[2 * (idx % 2) + uu][:, :], ALU.add),
                           "dEv")
            if h == 0:
                xh = xr[:, 2048 * Mt:2048 * Mt + 1024]
                wt(v, h_ev)
                h_mx0 = inc(v.reduce_max(cols[:, 32 + Mt:33 + Mt], xh, AX.X),
                            "selfV")
                h_mn0 = inc(v.tensor_reduce(mn0buf[:, Mt:Mt + 1], xh, AX.X,
                                            ALU.min), "selfV")
                hs0[Mt] = (h_mx0, h_mn0)

        def _eStats1(v, Mt):
            b = 16 * (Mt % 2)
            xrt = xr[:, 2048 * Mt:2048 * (Mt + 1)]
            v.wait_ge(S_("dEv"), 18 + 2 * Mt)
            if Mt >= 2:
                v.wait_ge(S_("gSr"), Mt - 1)
            h_x1 = inc(v.reduce_max(col(b + 5), xrt[:, 1024:2048], AX.X),
                       "selfV")
            h_n1 = inc(v.tensor_reduce(col(b + 6), xrt[:, 1024:2048], AX.X,
                                       ALU.min), "selfV")
            wt(v, hs0[Mt][0])
            wt(v, h_x1)
            h_mx = inc(v.tensor_tensor(cols[:, 48 + Mt:49 + Mt],
                                       cols[:, 32 + Mt:33 + Mt], col(b + 5),
                                       ALU.max), "selfV")
            wt(v, hs0[Mt][1])
            wt(v, h_n1)
            h_mn = inc(v.tensor_tensor(cols[:, 56 + Mt:57 + Mt],
                                       mn0buf[:, Mt:Mt + 1], col(b + 6),
                                       ALU.min), "selfV")
            h_bn = None
            for q4 in range(4):
                h_bn = inc(v.bn_stats(bnbuf[:, 6 * q4:6 * q4 + 6],
                                      xrt[:, 512 * q4:512 * (q4 + 1)]), "selfV")
            wt(v, h_bn)
            mv = bnbuf[:, 0:24].rearrange("p (g c) -> p c g", g=8, c=3)
            h_sm = inc(v.reduce_sum(cols[:, 32 + Mt:33 + Mt], mv[:, 1, :], AX.X),
                       "selfV")
            h_sq = inc(v.tensor_tensor(cols[:, 80:88], mv[:, 1, :], mv[:, 1, :],
                                       ALU.mult), "selfV")
            h_sv = inc(v.reduce_sum(col(b + 2), mv[:, 2, :], AX.X), "selfV")
            wt(v, h_sq)
            h_s2 = inc(v.reduce_sum(col(b + 4), cols[:, 80:88], AX.X), "selfV")
            wt(v, h_sv)
            h_cv = inc(v.tensor_scalar(col(b + 2), col(b + 2), 1.0 / 2048, EPS,
                                       ALU.mult, ALU.add), "selfV")
            wt(v, h_s2)
            wt(v, h_cv)
            h_ex = inc(v.scalar_tensor_tensor(col(b + 4), col(b + 4), 0.125,
                                              col(b + 2), ALU.mult, ALU.add),
                       "selfV")
            wt(v, h_sm)
            h_m = inc(v.tensor_scalar(cols[:, 32 + Mt:33 + Mt],
                                      cols[:, 32 + Mt:33 + Mt], 0.125, None,
                                      ALU.mult), "selfV")
            wt(v, h_m)
            h_m2 = inc(v.tensor_tensor(col(b + 2), cols[:, 32 + Mt:33 + Mt],
                                       cols[:, 32 + Mt:33 + Mt], ALU.mult),
                       "selfV")
            wt(v, h_m2)
            wt(v, h_ex)
            inc(v.tensor_tensor(col(b + 4), col(b + 4), col(b + 2),
                                ALU.subtract), "gVv")
            return (h_mx, h_mn)

        @blk.vector
        def _(v):
            hs = {}
            hmag = {}

            def _magic(t):
                v.wait_ge(S_("gU2"), t + 1)
                hmag[t] = inc(v.tensor_scalar_add(xu2[:, :], xu2[:, :], MAGIC),
                              "selfV")

            def _gXiC(t):
                if t >= 1:
                    v.wait_ge(S_("gGx"), t)
                else:
                    v.wait_ge(S_("dMM"), 12)
                wt(v, hmag[t])
                inc(v.tensor_scalar(xi32[:, :], xu2[:, :], MAGIC, None,
                                    ALU.subtract), "gXi")

            def _part2(t):
                b = 16 * (t % 2)
                v.wait_ge(S_("gSr"), t + 1)
                wt(v, hs[t][0])
                wt(v, hs[t][1])
                h_a = inc(v.tensor_tensor(col(b + 7), cols[:, 48 + t:49 + t],
                                          cols[:, 32 + t:33 + t],
                                          ALU.subtract), "selfV")
                h_b = inc(v.tensor_tensor(col(b + 8), cols[:, 32 + t:33 + t],
                                          cols[:, 56 + t:57 + t],
                                          ALU.subtract), "selfV")
                wt(v, h_a)
                wt(v, h_b)
                h_c = inc(v.tensor_tensor(col(b + 9), col(b + 7), col(b + 8),
                                          ALU.max), "selfV")
                wt(v, h_c)
                h_d = inc(v.tensor_tensor(col(b + 9), col(b + 9),
                                          cols[:, 40 + t:41 + t],
                                          ALU.mult), "selfV")
                wt(v, h_d)
                h_e = inc(v.tensor_scalar(col(b + 10), col(b + 9), Q, GUARD,
                                          ALU.mult, ALU.max), "selfV")
                wt(v, h_e)
                h_f = inc(v.reciprocal(col(b + 11), col(b + 10)), "selfV")
                h_i = inc(v.tensor_copy(cols[:, 48 + t:49 + t], col(b + 10)),
                          "selfV")
                wt(v, h_f)
                h_g = inc(v.tensor_tensor(cols[:, 40 + t:41 + t],
                                          cols[:, 40 + t:41 + t],
                                          col(b + 11), ALU.mult), "selfV")
                wt(v, h_g)
                wt(v, h_i)
                inc(v.scalar_tensor_tensor(cols[:, 32 + t:33 + t],
                                           cols[:, 32 + t:33 + t], -1.0,
                                           cols[:, 40 + t:41 + t], ALU.mult,
                                           ALU.mult), "gVb")

            for i in range(2 * NC):
                ft = i // 2
                if i >= 4 and i % 2 == 0:
                    _dEvac(v, (i - 4) // 2)
                    if i >= 20:
                        hs[(i - 20) // 2] = _eStats1(v, (i - 20) // 2)
                v.wait_ge(S_(f"fKt2{i % 2}"), 16 * (i // 2 + 1))
                wt(v, h_sk[ft])
                h_qb = inc(v.tensor_scalar(ktb[i % 2][:, :], ktb[i % 2][:, :],
                                           cols[:, 64 + ft:65 + ft], MAGIC,
                                           ALU.mult, ALU.add), "selfV")
                wt(v, h_qb)
                h_kv = inc(v.tensor_scalar(kti[i % 2][:, :], ktb[i % 2][:, :],
                                           MAGIC, None, ALU.subtract), "fKv")
                if i >= 3:
                    v.wait_ge(S_("fFt"), 2 * i - 4)
                wt(v, h_kv)
                inc(v.tensor_scalar_mul(kqb[i % 3][:, :], kti[i % 2][:, :],
                                        cols[:, 48 + ft:49 + ft]), "fKg")
                if i == 24:
                    _part2(0)
                elif i == 26:
                    _part2(1)
            for j in range(14, 16):
                _dEvac(v, j)

            for Mt in range(NT + 2):
                if Mt < NT:
                    if Mt >= 1:
                        if Mt + 1 < NT:
                            _part2(Mt + 1)
                        _magic(Mt)
                        _gXiC(Mt)
                    if Mt == 0:
                        _magic(0)
                        _gXiC(0)
                        hs[6] = _eStats1(v, 6)
                        hs[7] = _eStats1(v, 7)
                        v.wait_ge(S_("dMM"), 16)
                        v.wait_ge(S_("gBb"), 16)
                        h1 = inc(v.reduce_max(col(81), bias_b[:, :], AX.X,
                                              apply_absolute_value=True), "selfV")
                        wt(v, h1)
                        h2 = inc(v.tensor_scalar(col(82), col(81), Q, GUARD, ALU.mult,
                                                 ALU.max), "selfV")
                        wt(v, h2)
                        h3 = inc(v.reciprocal(col(83), col(82)), "selfV")
                        wt(v, h3)
                        h4 = inc(v.tensor_scalar(junkG2[:, :], bias_b[:, :], col(83), MAGIC,
                                                 ALU.mult, ALU.add), "selfV")
                        wt(v, h4)
                        h5 = inc(v.tensor_scalar(bqi[:, :], junkG2[:, :], MAGIC, None,
                                                 ALU.subtract), "selfV")
                        wt(v, h5)
                        inc(v.tensor_scalar(bqb[:, :], bqi[:, :], col(82), None, ALU.mult),
                            "gBq")

                k = Mt - 2
                if k == NT - 1:
                    b = 16 * (k % 2)
                    v.wait_ge(S_("gP1"), 2 * k + 2)
                    h_y0 = inc(v.tensor_tensor(ttile[:, 0:1024], ttile[:, 0:1024],
                                               bqb[:, 0:1024], ALU.add), "selfV")
                    wt(v, h_y0)
                    h_y0m = inc(v.reduce_max(col(b + 13), ttile[:, 0:1024],
                                             AX.X), "selfV")
                    for u in range(2, 4):
                        v.wait_ge(S_("gMM"), 4 * k + u + 1)
                        inc(v.tensor_scalar(ttile[:, 512 * u:512 * (u + 1)],
                                            pF[u][:, :], cols[:, 48 + k:49 + k],
                                            None, ALU.mult), "gP1V")
                    v.wait_ge(S_("gP1V"), 2 * k + 2)
                    h_y1 = inc(v.tensor_tensor(ttile[:, 1024:2048],
                                               ttile[:, 1024:2048],
                                               bqb[:, 1024:2048], ALU.add),
                               "selfV")
                    wt(v, h_y1)
                    h_y1m = inc(v.reduce_max(col(b + 14), ttile[:, 1024:2048],
                                             AX.X), "selfV")
                    wt(v, h_y0m)
                    wt(v, h_y1m)
                    h_ym = inc(v.tensor_tensor(col(b + 13), col(b + 13),
                                               col(b + 14), ALU.max), "selfV")
                    wt(v, h_ym)
                    h_c0 = inc(v.tensor_single_scalar(col(b + 13), col(b + 13),
                                                      0.0, ALU.max), "selfV")
                    wt(v, h_c0)
                    h_sy = inc(v.tensor_scalar(cols[:, 56 + k:57 + k], col(b + 13),
                                               Q, GUARD, ALU.mult, ALU.max),
                               "selfV")
                    wt(v, h_sy)
                    h_ry = inc(v.reciprocal(cols[:, 64 + k:65 + k],
                                            cols[:, 56 + k:57 + k]), "selfV")
                    v.wait_ge(S_("gOutR"), k)
                    wt(v, h_ry)
                    for hh2 in range(2):
                        cs = slice(1024 * hh2, 1024 * (hh2 + 1))
                        h_z = inc(v.tensor_scalar(junkG2[:, cs], ttile[:, cs],
                                                  cols[:, 64 + k:65 + k],
                                                  MAGIC, ALU.mult, ALU.add),
                                  "gVz")
                        wt(v, h_z)
                        h_yi = inc(v.tensor_scalar(itile[:, cs],
                                                   junkG2[:, cs], MAGIC,
                                                   None, ALU.subtract),
                                   "selfV")
                        wt(v, h_yi)
                        inc(v.tensor_scalar(junkG2[:, cs], itile[:, cs],
                                            cols[:, 56 + k:57 + k], 0.0,
                                            ALU.mult, ALU.max), "gV1h")
                elif k >= 0:
                    b = 16 * (k % 2)
                    for u in range(2, 4):
                        v.wait_ge(S_("gMM"), 4 * k + u + 1)
                        inc(v.tensor_scalar(ttile[:, 512 * u:512 * (u + 1)],
                                            pF[u][:, :], cols[:, 48 + k:49 + k],
                                            None, ALU.mult), "gP1V")
                    v.wait_ge(S_("gP1"), 2 * k + 2)
                    v.wait_ge(S_("gP1V"), 2 * k + 2)
                    h_y = inc(v.tensor_tensor(ttile[:, :], ttile[:, :],
                                              bqb[:, :], ALU.add), "selfV")
                    wt(v, h_y)
                    h_ym = inc(v.reduce_max(col(b + 13), ttile[:, :], AX.X),
                               "selfV")
                    wt(v, h_ym)
                    h_c0 = inc(v.tensor_single_scalar(col(b + 13), col(b + 13),
                                                      0.0, ALU.max), "selfV")
                    wt(v, h_c0)
                    h_sy = inc(v.tensor_scalar(cols[:, 56 + k:57 + k], col(b + 13),
                                               Q, GUARD, ALU.mult, ALU.max),
                               "selfV")
                    wt(v, h_sy)
                    h_ry = inc(v.reciprocal(cols[:, 64 + k:65 + k],
                                            cols[:, 56 + k:57 + k]), "selfV")
                    if k >= 1:
                        v.wait_ge(S_("gOutR"), k)
                    wt(v, h_ry)
                    h_z = inc(v.tensor_scalar(junkG2[:, :], ttile[:, :],
                                              cols[:, 64 + k:65 + k], MAGIC,
                                              ALU.mult, ALU.add), "gVz")
                    wt(v, h_z)
                    h_yi = inc(v.tensor_scalar(itile[:, :], junkG2[:, :],
                                               MAGIC, None, ALU.subtract),
                               "selfV")
                    wt(v, h_yi)
                    inc(v.tensor_scalar(junkG2[:, :], itile[:, :],
                                        cols[:, 56 + k:57 + k], 0.0,
                                        ALU.mult, ALU.max), "gV1")


        def _sLnexp(s, Mt):
            b = 16 * (Mt % 2)
            s.wait_ge(S_("gVv"), Mt + 1)
            h_ln = inc(s.activation(col(b + 2), col(b + 4), AF.Ln), "selfS")
            wt(s, h_ln)
            inc(s.activation(cols[:, 40 + Mt:41 + Mt], col(b + 2), AF.Exp,
                             scale=-0.5), "gSr")

        @blk.scalar
        def _(s):
            def _u2(t):
                s.wait_ge(S_("gVb"), t + 1)
                if t == 0:
                    s.wait_ge(S_("dMM"), 10)
                else:
                    s.wait_ge(S_("gXi"), t)
                inc(s.activation(xu2[:, :], xr[:, 2048 * t:2048 * (t + 1)],
                                 AF.Identity, bias=cols[:, 32 + t:33 + t],
                                 scale=cols[:, 40 + t:41 + t]), "gU2")

            def _gGx(t):
                s.wait_ge(S_("gXi"), t + 1)
                if t >= 1:
                    s.wait_ge(S_("gTp"), 4 * t)
                else:
                    s.wait_ge(S_("dMM"), 13)
                inc(s.copy(qx[:, :], xi32[:, :]), "gGx")

            for fidx in range(2 * NC):
                ft, hh = fidx // 2, fidx % 2
                for b2 in range(2):
                    slot = (2 * fidx + b2) % 4
                    s.wait_ge(S_("fFt"), 2 * fidx + b2 + 1)
                    dst = qk.rearrange("p (c f) -> p c f", c=NC)[
                        :, 8 * hh + 4 * b2:8 * hh + 4 * b2 + 4,
                        128 * ft:128 * (ft + 1)]
                    src = pH[slot][:, :].rearrange("p (c f) -> p c f", c=4)
                    inc(ecopy(s, dst, src), "fFe")
                if fidx == 28:
                    _u2(0)
                if fidx % 2 == 0 and 18 <= fidx <= 28:
                    _sLnexp(s, (fidx - 18) // 2)
            for Mt in range(NT + 1):
                if Mt < NT:
                    _gGx(Mt)
                    if Mt + 1 < NT:
                        _u2(Mt + 1)
                    if Mt in (0, 1):
                        _sLnexp(s, 6 + Mt)
                    if Mt >= 2:
                        s.wait_ge(S_("gMM"), 4 * Mt - 4)
                    for bb in range(4):
                        s.wait_ge(S_("gTp"), 4 * Mt + bb + 1)
                        if Mt == 0 and bb == 0:
                            s.wait_ge(S_("dMM"), 16)
                        inc(s.copy(xqs[Mt % 2][:, 512 * bb:512 * (bb + 1)],
                                   pH[bb][:, :]), "gXq")
                k = Mt - 1
                if k >= 0:
                    if k == 0:
                        s.wait_ge(S_("gBq"), 1)
                    else:
                        s.wait_ge(S_("gVz"), k)
                    for u in range(2):
                        s.wait_ge(S_("gMM"), 4 * k + u + 1)
                        inc(s.activation(ttile[:, 512 * u:512 * (u + 1)],
                                         pF[u][:, :], AF.Copy,
                                         scale=cols[:, 48 + k:49 + k]), "gP1")

    if PHASES < 4:
        with nc.Block() as blk:
            @blk.sync
            def _(sync):
                for Mt in range(NT):
                    inc(sync.dma_start(d_out[128 * Mt:128 * (Mt + 1), :],
                                       xr[:, 2048 * Mt:2048 * (Mt + 1)]),
                        "zOut", 16)
                sync.wait_ge(S_("zOut"), 16 * NT)
    return nc


_CACHE = {}


def _host_inputs(L0_attn_x, L0_attn_w_qkv, L0_attn_w_proj, L0_ffn_kernel,
                 L0_ffn_bias):
    x = np.ascontiguousarray(np.asarray(L0_attn_x, dtype=np.float32)[0])
    WqkvT = np.ascontiguousarray(np.asarray(L0_attn_w_qkv, dtype=np.float32).T)
    WpT = np.ascontiguousarray(np.asarray(L0_attn_w_proj, dtype=np.float32).T)
    WkT = np.ascontiguousarray(np.asarray(L0_ffn_kernel, dtype=np.float32).T)
    biasr = np.ascontiguousarray(
        np.asarray(L0_ffn_bias, dtype=np.float32).reshape(1, D))
    ident = np.eye(128, dtype=np.float32).astype(BF)
    m = np.zeros((128, 128), dtype=np.float32)
    for t in range(8):
        m[t * 16:(t + 1) * 16, t * 16:(t + 1) * 16] = 1.0
    mask4 = np.ascontiguousarray(np.tile(m, (1, 4))).astype(BF)
    g = np.arange(64)
    h = np.arange(16)
    in_maps, rows_all = [], []
    for k in range(8):
        rows = (h[None, :] * 512 + 64 * k + g[:, None]).reshape(-1)
        rows_all.append(rows)
        in_maps.append({
            "x_att": np.ascontiguousarray(x[1024 * k:1024 * (k + 1)]),
            "x_res": np.ascontiguousarray(x[rows]),
            "wqkvT": WqkvT, "wpT": WpT, "wkT": WkT, "bias": biasr,
            "ident": ident, "mask": mask4,
        })
    return in_maps, rows_all


def kernel(L0_attn_x, L0_attn_w_qkv, L0_attn_w_proj, L0_ffn_kernel, L0_ffn_bias):
    in_maps, rows_all = _host_inputs(L0_attn_x, L0_attn_w_qkv, L0_attn_w_proj,
                                     L0_ffn_kernel, L0_ffn_bias)
    key = PHASES
    if key not in _CACHE:
        _CACHE[key] = build_program()
    res = run_bass_kernel_spmd(_CACHE[key], in_maps, core_ids=list(range(8)))
    out = np.empty((1, 8192, 2048), dtype=np.float32)
    for k in range(8):
        out[0, rows_all[k]] = res.results[k]["out"]
    return out


# version 26 (final)
# speedup vs baseline: 1.0405x; 1.0204x over previous
import numpy as np
import ml_dtypes
import concourse.bass as bass
import concourse.mybir as mybir
from concourse.bass_utils import run_bass_kernel_spmd

BF = ml_dtypes.bfloat16
F32 = mybir.dt.float32
BF16 = mybir.dt.bfloat16
I32 = mybir.dt.int32
AF = mybir.ActivationFunctionType
ALU = mybir.AluOpType
AX = mybir.AxisListType

NL, D, H, HD, E = 1024, 2048, 16, 128, 6144
NT, NC, NE, NG = 8, 16, 48, 32
SCALE = float(HD) ** -0.5
EPS = 1e-5
GUARD = 1e-30
Q = 1.0 / 127.0
MAGIC = 12582912.0
INV_D = 1.0 / D

PHASES = 5


def build_program():
    nc = bass.Bass("TRN2", target_bir_lowering=False)

    d_xatt = nc.dram_tensor("x_att", [NL, D], F32, kind="ExternalInput").ap()
    d_xres = nc.dram_tensor("x_res", [NL, D], F32, kind="ExternalInput").ap()
    d_wqkvT = nc.dram_tensor("wqkvT", [D, E], F32, kind="ExternalInput").ap()
    d_wpT = nc.dram_tensor("wpT", [D, D], F32, kind="ExternalInput").ap()
    d_wkT = nc.dram_tensor("wkT", [D, D], F32, kind="ExternalInput").ap()
    d_bias = nc.dram_tensor("bias", [1, D], F32, kind="ExternalInput").ap()
    d_ident = nc.dram_tensor("ident", [128, 128], BF16, kind="ExternalInput").ap()
    d_mask = nc.dram_tensor("mask", [128, 512], BF16, kind="ExternalInput").ap()
    d_out = nc.dram_tensor("out", [NL, D], F32, kind="ExternalOutput").ap()

    BASE = 16640

    def A(name, shape, dt, offset):
        return nc.alloc_sbuf_tensor_at(name, shape, dt, offset=BASE + offset)

    aT = A("aT", [128, 16384], BF16, offset=0)
    xr = A("xr", [128, 16384], F32, offset=0)
    qT = A("qT", [128, 16384], BF16, offset=65536)
    kT = A("kT", [128, 16384], BF16, offset=98304)
    vT = A("vT", [128, 16384], BF16, offset=131072)
    xbuf = [A(f"xbuf{i}", [128, 2048], F32, offset=65536 + 8192 * i) for i in range(2)]
    abuf = [A(f"abuf{i}", [128, 2048], BF16, offset=81920 + 4096 * i) for i in range(2)]
    junkA = A("junkA", [128, 2048], F32, offset=90112)
    qk = A("qk", [128, 32768], BF16, offset=65536)
    wpb = A("wpb", [128, 16384], BF16, offset=131072)
    xqs = [A(f"xqs{i}", [128, 2048], BF16, offset=131072 + 4096 * i) for i in range(2)]
    itile = A("itile", [128, 2048], I32, offset=139264)
    bqi = A("bqi", [128, 2048], I32, offset=139264)
    junkG2 = A("junkG2", [128, 2048], F32, offset=147456)
    ttile = A("ttile", [128, 2048], F32, offset=155648)
    bias_b = A("bias_b", [128, 2048], F32, offset=155648)
    sT = A("sT", [128, 16384], BF16, offset=163840)
    xu2 = A("xu2", [128, 2048], F32, offset=163840)
    xi32 = A("xi32", [128, 2048], I32, offset=172032)
    qx = A("qx", [128, 2048], BF16, offset=180224)
    bqb = A("bqb", [128, 2048], F32, offset=184320)
    wqb = [A(f"wqb{i}", [128, 2048], BF16, offset=198240 + 4096 * i) for i in range(3)]
    Ebuf = [A(f"Ebuf{i}", [128, 512], BF16, offset=0 + 1024 * i) for i in range(2)]
    Embuf = [A(f"Embuf{i}", [128, 512], BF16, offset=2048 + 1024 * i) for i in range(2)]
    Abuf = [A(f"Abuf{i}", [128, 512], BF16, offset=4096 + 1024 * i) for i in range(2)]
    aTTbuf = [A(f"aTTbuf{i}", [128, 512], BF16, offset=6144 + 1024 * i) for i in range(2)]
    V8buf = [A(f"V8buf{i}", [128, 512], BF16, offset=8192 + 1024 * i) for i in range(2)]
    ktb = [A(f"ktb{i}", [128, 1024], F32, offset=198240 + 4096 * i) for i in range(2)]
    ktc = [A(f"ktc{i}", [128, 1024], F32, offset=32768 + 4096 * i) for i in range(2)]
    kti = [A(f"kti{i}", [128, 1024], I32, offset=198240 + 4096 * i) for i in range(2)]
    kqb = [A(f"kqb{i}", [128, 1024], BF16, offset=206432 + 2048 * i) for i in range(3)]
    ident = A("ident", [128, 128], BF16, offset=196608)
    mask4 = A("mask4", [128, 512], BF16, offset=196864)
    cols = A("cols", [128, 88], F32, offset=197888)
    bnbuf = A("bnbuf", [128, 24], F32, offset=212608)
    mn0buf = A("mn0buf", [128, 8], F32, offset=212576)

    def C_S4(p):
        return cols[:, 32 + 4 * p:36 + 4 * p]

    def C_R4(p):
        return cols[:, 40 + 4 * p:44 + 4 * p]

    pF = [nc.alloc_psum_tensor(f"pF{i}", [128, 512], F32) for i in range(4)]
    pH = [nc.alloc_psum_tensor(f"pH{i}", [128, 512], BF16) for i in range(4)]

    sems, cnt = {}, {}

    def S_(name):
        if name not in sems:
            sems[name] = nc.semaphore(name).__enter__()
            cnt[name] = 0
        return sems[name]

    def inc(instr, name, k=1):
        h = S_(name)
        instr.then_inc(h, k)
        cnt[name] += k
        return (name, cnt[name])

    def wt(eng, handle):
        eng.wait_ge(S_(handle[0]), handle[1])

    def col(i):
        return cols[:, i:i + 1]

    def ecopy(eng, dst, srcap):
        if hasattr(eng, "tensor_copy"):
            return eng.tensor_copy(dst, srcap)
        return eng.copy(dst, srcap)

    h_sk = {}
    with nc.Block() as blk:
        def _bEvac(eng, nb, semname, extra=None):
            for e in range(NE):
                if extra is not None and 8 <= e < 40:
                    extra(eng, e - 8)
                eng.wait_ge(S_("bMM"), e + 1)
                sec, head = e // 16, e % 16
                dstT = (qT, kT, vT)[sec]
                dst = dstT.rearrange("p (m b t i) -> p i b t m",
                                     m=16, b=8, t=8, i=16)[
                    :, head, 4 * nb:4 * nb + 4, :, :]
                src = pF[2 * (e % 2) + nb][:, :].rearrange(
                    "p (b t m) -> p b t m", b=4, t=8, m=16)
                inc(ecopy(eng, dst, src), semname)

        def _p1unit(v, i):
            ft, hh = i // 2, i % 2
            v.wait_ge(S_(f"fKt1{i % 2}"), 16 * (i // 2 + 1))
            h1 = inc(v.reduce_max(col(84 + hh), ktc[i % 2][:, :], AX.X,
                                  apply_absolute_value=True), "fK1")
            if hh == 1:
                wt(v, h1)
                h_cb = inc(v.tensor_tensor(col(84), col(84), col(85),
                                           ALU.max), "selfV")
                wt(v, h_cb)
                h_s = inc(v.tensor_scalar(cols[:, 48 + ft:49 + ft],
                                          col(84), Q, GUARD, ALU.mult,
                                          ALU.max), "selfV")
                wt(v, h_s)
                h_sk[ft] = inc(v.reciprocal(cols[:, 64 + ft:65 + ft],
                                            cols[:, 48 + ft:49 + ft]),
                               "selfV")

        @blk.sync
        def _(sync):
            inc(sync.dma_start(ident[:, :], d_ident[:, :]), "aCn", 16)
            inc(sync.dma_start(mask4[:, :], d_mask[:, :]), "aCn", 16)
            for t in range(NT):
                if t >= 2:
                    sync.wait_ge(S_("aVn"), t - 1)
                    sync.wait_ge(S_("aVv"), t - 1)
                inc(sync.dma_start(xbuf[t % 2][:, :],
                                   d_xatt[128 * t:128 * (t + 1), :]),
                    f"aLd{t % 2}", 16)
            for idx in range(2 * NC):
                ft, hh = idx // 2, idx % 2
                if idx >= 2:
                    sync.wait_ge(S_("fK1"), idx - 1)
                inc(sync.dma_start(ktc[idx % 2][:, :],
                                   d_wkT[128 * ft:128 * (ft + 1),
                                         1024 * hh:1024 * (hh + 1)]),
                    f"fKt1{idx % 2}", 16)

        def _aEvac(s, t):
            for bb in range(4):
                bq = 4 * t + bb
                s.wait_ge(S_("aTt"), bq + 1)
                dst = aT.rearrange("p (c n) -> p c n", c=NC)[
                    :, 4 * bb:4 * bb + 4, 128 * t:128 * (t + 1)]
                src = pH[bq % 4][:, :].rearrange("p (c n) -> p c n", c=4)
                inc(ecopy(s, dst, src), "aEv")

        @blk.scalar
        def _(s):
            for t in range(NT):
                b = 16 * (t % 2)
                s.wait_ge(S_("aVv"), t + 1)
                h_ln = inc(s.activation(col(b + 2), col(b + 4), AF.Ln), "selfS")
                wt(s, h_ln)
                h_rs = inc(s.activation(col(b + 5), col(b + 2), AF.Exp,
                                        scale=-0.5), "aSr")
                wt(s, h_rs)
                h_b1 = inc(s.mul(col(b + 6), col(b + 3), col(b + 5)), "selfS")
                wt(s, h_b1)
                h_b2 = inc(s.activation(col(b + 6), col(b + 6), AF.Identity,
                                        scale=-1.0), "selfS")
                wt(s, h_b2)
                if t >= 2:
                    s.wait_ge(S_("aTt"), 4 * t - 4)
                inc(s.activation(abuf[t % 2][:, :], xbuf[t % 2][:, :], AF.Identity,
                                 bias=col(b + 6), scale=col(b + 5)), "aVn")
                if t >= 1:
                    _aEvac(s, t - 1)
            _aEvac(s, NT - 1)
            _bEvac(s, 1, "bEvS")

        @blk.vector
        def _(v):
            for t in range(NT):
                v.wait_ge(S_(f"aLd{t % 2}"), 16 * (t // 2 + 1))
                b = 16 * (t % 2)
                xbt = xbuf[t % 2]
                h_bn = None
                for q4 in range(4):
                    h_bn = inc(v.bn_stats(bnbuf[:, 6 * q4:6 * q4 + 6],
                                          xbt[:, 512 * q4:512 * (q4 + 1)]),
                               "selfV")
                wt(v, h_bn)
                mv = bnbuf[:, 0:24].rearrange("p (g c) -> p c g", g=8, c=3)
                h_sm = inc(v.reduce_sum(col(b + 3), mv[:, 1, :], AX.X), "selfV")
                h_sq = inc(v.tensor_tensor(cols[:, 80:88], mv[:, 1, :],
                                           mv[:, 1, :], ALU.mult), "selfV")
                h_sv = inc(v.reduce_sum(col(b + 2), mv[:, 2, :], AX.X), "selfV")
                wt(v, h_sq)
                h_s2 = inc(v.reduce_sum(col(b + 4), cols[:, 80:88], AX.X),
                           "selfV")
                wt(v, h_sv)
                h_cv = inc(v.tensor_scalar(col(b + 2), col(b + 2), 1.0 / 2048,
                                           EPS, ALU.mult, ALU.add), "selfV")
                wt(v, h_s2)
                wt(v, h_cv)
                h_ex = inc(v.scalar_tensor_tensor(col(b + 4), col(b + 4), 0.125,
                                                  col(b + 2), ALU.mult, ALU.add),
                           "selfV")
                wt(v, h_sm)
                h_m = inc(v.tensor_scalar(col(b + 3), col(b + 3), 0.125, None,
                                          ALU.mult), "selfV")
                wt(v, h_m)
                h_m2 = inc(v.tensor_tensor(col(b + 2), col(b + 3), col(b + 3),
                                           ALU.mult), "selfV")
                wt(v, h_m2)
                wt(v, h_ex)
                inc(v.tensor_tensor(col(b + 4), col(b + 4), col(b + 2),
                                    ALU.subtract), "aVv")

            inc(v.memset(col(80), -12.0), "aC12")
            _bEvac(v, 0, "bEvV", _p1unit)

        @blk.tensor
        def _(tn):
            def _bHalf(e, nb, bank):
                tn.wait_ge(S_(f"bWq{e % 3}"), 16 * (e // 3 + 1))
                for c in range(NC):
                    wch = wqb[e % 3][:, 128 * c:128 * (c + 1)]
                    mm = tn.matmul(pF[bank][:, :], wch,
                                   aT[:, 1024 * c + 512 * nb:
                                      1024 * c + 512 * (nb + 1)],
                                   start=(c == 0), stop=(c == NC - 1))
                return mm

            tn.wait_ge(S_("aCn"), 32)
            for t in range(NT):
                tn.wait_ge(S_("aVn"), t + 1)
                for bb in range(4):
                    bq = 4 * t + bb
                    if bq >= 4:
                        tn.wait_ge(S_("aEv"), bq - 3)
                    for j in range(4):
                        c = 4 * bb + j
                        tr = tn.transpose(pH[bq % 4][:, 128 * j:128 * (j + 1)],
                                          abuf[t % 2][:, 128 * c:128 * (c + 1)],
                                          ident[:, :])
                    inc(tr, "aTt")
                if t == 5:
                    tn.wait_ge(S_("aEv"), 16)
                    _bHalf(0, 0, 0)
                    _bHalf(1, 0, 2)
            tn.wait_ge(S_("aEv"), 32)
            for e in range(2):
                mm = _bHalf(e, 1, 2 * e + 1)
                inc(mm, "bMM")
            for e in range(2, NE):
                tn.wait_ge(S_(f"bWq{e % 3}"), 16 * (e // 3 + 1))
                tn.wait_ge(S_("bEvV"), e - 1)
                tn.wait_ge(S_("bEvS"), e - 1)
                pa, pb = pF[2 * (e % 2)], pF[2 * (e % 2) + 1]
                for c in range(NC):
                    wch = wqb[e % 3][:, 128 * c:128 * (c + 1)]
                    st, sp = (c == 0), (c == NC - 1)
                    tn.matmul(pa[:, :], wch, aT[:, 1024 * c:1024 * c + 512],
                              start=st, stop=sp)
                    mm = tn.matmul(pb[:, :], wch,
                                   aT[:, 1024 * c + 512:1024 * c + 1024],
                                   start=st, stop=sp)
                inc(mm, "bMM")

        @blk.gpsimd
        def _(g):
            for e in range(NE):
                if e >= 3:
                    g.wait_ge(S_("bMM"), e - 2)
                src = d_wqkvT[:, 128 * e:128 * (e + 1)].rearrange(
                    "(c p) e -> p c e", p=128)
                dst = wqb[e % 3][:, :].rearrange("p (c e) -> p c e", c=NC)
                inc(g.dma_start(dst, src), f"bWq{e % 3}", 16)

    if PHASES >= 3:
     def slab(m, b):
        return 128 * (8 * m + b)

     with nc.Block() as blk:
        @blk.tensor
        def _(tn):
            for g in range(NG + 1):
                if g < NG:
                    m, cg = g // 2, g % 2
                    if g >= 2:
                        tn.wait_ge(S_("cV8e"), g - 1)
                    for j in range(4):
                        sl = slab(m, 4 * cg + j)
                        tr = tn.transpose(pH[2 + g % 2][:, 128 * j:128 * (j + 1)],
                                          vT[:, sl:sl + 128], ident[:, :])
                    inc(tr, "cV8t")
                    if g >= 2:
                        tn.wait_ge(S_("cExp"), g - 1)
                    for j in range(4):
                        sl = slab(m, 4 * cg + j)
                        mm = tn.matmul(pF[g % 2][:, 128 * j:128 * (j + 1)],
                                       qT[:, sl:sl + 128], kT[:, sl:sl + 128],
                                       start=True, stop=True)
                    inc(mm, "cA8")
                k = g - 1
                if k >= 0:
                    tn.wait_ge(S_("cAsc"), k + 1)
                    if k >= 2:
                        tn.wait_ge(S_("cT1e"), k - 1)
                    for j in range(4):
                        tr = tn.transpose(pH[k % 2][:, 128 * j:128 * (j + 1)],
                                          Abuf[k % 2][:, 128 * j:128 * (j + 1)],
                                          ident[:, :])
                    inc(tr, "cAtT")
                    tn.wait_ge(S_("cT1e"), k + 1)
                    tn.wait_ge(S_("cV8e"), k + 1)
                    if k >= 2:
                        tn.wait_ge(S_("cO8e"), k - 1)
                    for j in range(4):
                        mm = tn.matmul(pF[2 + k % 2][:, 128 * j:128 * (j + 1)],
                                       V8buf[k % 2][:, 128 * j:128 * (j + 1)],
                                       aTTbuf[k % 2][:, 128 * j:128 * (j + 1)],
                                       start=True, stop=True)
                    inc(mm, "cO8")

        @blk.scalar
        def _(s):
            s.wait_ge(S_("aC12"), 1)
            for g in range(NG + 1):
                if g < NG:
                    s.wait_ge(S_("cA8"), g + 1)
                    if g >= 2:
                        s.wait_ge(S_("cAsc"), g - 1)
                    inc(s.activation(Ebuf[g % 2][:, :], pF[g % 2][:, :], AF.Exp,
                                     bias=col(80), scale=SCALE), "cExp")

                k = g - 1
                if k >= 0:
                    s.wait_ge(S_("cAtT"), k + 1)
                    if k >= 2:
                        s.wait_ge(S_("cO8"), k - 1)
                    inc(s.copy(aTTbuf[k % 2][:, :], pH[k % 2][:, :]), "cT1e")
                    s.wait_ge(S_("cO8"), k + 1)
                    km, kc = k // 2, k % 2
                    dst = sT.rearrange("p (t x) -> p t x", t=8)[
                        :, 4 * kc:4 * kc + 4, 128 * km:128 * (km + 1)]
                    inc(s.copy(
                        dst, pF[2 + k % 2][:, :].rearrange("p (j l) -> p j l", j=4)),
                        "cO8e")

        @blk.vector
        def _(v):
            for g in range(NG + 1):
                if g < NG:
                    p = g % 2
                    v.wait_ge(S_("cV8t"), g + 1)
                    if g >= 2:
                        v.wait_ge(S_("cO8"), g - 1)
                    inc(v.tensor_copy(V8buf[g % 2][:, :], pH[2 + g % 2][:, :]),
                        "cV8e")
                    v.wait_ge(S_("cExp"), g + 1)
                    h_mask = inc(v.tensor_tensor(Embuf[p][:, :], Ebuf[p][:, :],
                                                 mask4[:, :], ALU.mult), "selfV")
                    wt(v, h_mask)
                    h_rs = inc(v.reduce_sum(
                        C_S4(p), Embuf[p][:, :].rearrange("p (b n) -> p b n", b=4),
                        AX.X), "selfV")
                    wt(v, h_rs)
                    h_rc = inc(v.reciprocal(C_R4(p), C_S4(p)), "selfV")
                    if g >= 2:
                        v.wait_ge(S_("cAtT"), g - 1)
                    wt(v, h_rc)
                    for j in range(4):
                        instr = v.tensor_scalar(Abuf[p][:, 128 * j:128 * (j + 1)],
                                                Embuf[p][:, 128 * j:128 * (j + 1)],
                                                C_R4(p)[:, j:j + 1], None, ALU.mult)
                        if j < 3:
                            inc(instr, "selfV")
                    inc(instr, "cAsc")



        @blk.gpsimd
        def _(g):
            for m in range(NC):
                g.wait_ge(S_("cV8t"), 2 * m + 2)
                if m >= 4:
                    g.wait_ge(S_(f"dWp{m % 4}"), 16 * (m // 4))
                src = d_wpT[128 * m:128 * (m + 1), 0:1024]
                inc(g.dma_start(wpb[:, 1024 * m:1024 * (m + 1)], src),
                    f"dWp{m % 4}", 16)
            for m in range(NC):
                g.wait_ge(S_("dMM"), 8)
                if m >= 4:
                    g.wait_ge(S_(f"dWp{m % 4}"), 64 + 16 * (m // 4))
                src = d_wpT[128 * m:128 * (m + 1), 1024:2048]
                inc(g.dma_start(wpb[:, 1024 * m:1024 * (m + 1)], src),
                    f"dWp{m % 4}", 16)
            for k in range(NT - 1):
                g.wait_ge(S_("gV1"), k + 1)
                inc(g.tensor_tensor(xr[:, 2048 * k:2048 * (k + 1)],
                                    xr[:, 2048 * k:2048 * (k + 1)],
                                    junkG2[:, :], ALU.add), "gOutR")
            for hh2 in range(2):
                g.wait_ge(S_("gV1h"), hh2 + 1)
                c0 = 2048 * (NT - 1) + 1024 * hh2
                inc(g.tensor_tensor(xr[:, c0:c0 + 1024], xr[:, c0:c0 + 1024],
                                    junkG2[:, 1024 * hh2:1024 * (hh2 + 1)],
                                    ALU.add), "gOutRh")

        @blk.sync
        def _(sync):
            _xq = {2: 0, 3: 0, 4: 1, 5: 1, 6: 2, 7: 2, 0: 3, 1: 3}
            for t in (2, 3, 4, 5, 6, 7, 0, 1):
                if t < 2:
                    sync.wait_ge(S_("cO8e"), NG)
                if _xq[t] >= 1:
                    sync.wait_ge(S_(f"dXr{t % 2}"), 16 * _xq[t])
                inc(sync.dma_start(xr[:, 2048 * t:2048 * (t + 1)],
                                   d_xres[128 * t:128 * (t + 1), :]),
                    f"dXr{t % 2}", 16)
            for idx in range(2 * NC):
                ft, hh = idx // 2, idx % 2
                sync.wait_ge(S_("fK1"), 2 * NC)
                if idx >= 2:
                    sync.wait_ge(S_("fKv"), idx - 1)
                    sync.wait_ge(S_("fKg"), idx - 1)
                inc(sync.dma_start(ktb[idx % 2][:, :],
                                   d_wkT[128 * ft:128 * (ft + 1),
                                         1024 * hh:1024 * (hh + 1)]),
                    f"fKt2{idx % 2}", 16)
            sync.wait_ge(S_("dMM"), 16)
            inc(sync.dma_start(bias_b[:, :], d_bias.partition_broadcast(128)),
                "gBb", 16)
            for Mt in range(NT - 1):
                sync.wait_ge(S_("gOutR"), Mt + 1)
                inc(sync.dma_start(d_out[128 * Mt:128 * (Mt + 1), :],
                                   xr[:, 2048 * Mt:2048 * (Mt + 1)]), "gOut", 16)
            for hh2 in range(2):
                sync.wait_ge(S_("gOutRh"), hh2 + 1)
                c0 = 2048 * (NT - 1) + 1024 * hh2
                inc(sync.dma_start(
                    d_out[128 * (NT - 1):128 * NT, 1024 * hh2:1024 * (hh2 + 1)],
                    xr[:, c0:c0 + 1024]), "gOut", 16)
            sync.wait_ge(S_("gOut"), 16 * (NT + 1))

        @blk.tensor
        def _(tn):
            def _fT(fidx):
                tn.wait_ge(S_("fKg"), fidx + 1)
                for b2 in range(2):
                    slot = (2 * fidx + b2) % 4
                    if 2 * fidx + b2 >= 4:
                        tn.wait_ge(S_("fFe"), 2 * fidx + b2 - 3)
                    for j in range(4):
                        cc = 4 * b2 + j
                        tr = tn.transpose(
                            pH[slot][:, 128 * j:128 * (j + 1)],
                            kqb[fidx % 3][:, 128 * cc:128 * (cc + 1)],
                            ident[:, :])
                    inc(tr, "fFt")

            tn.wait_ge(S_("cO8e"), NG)
            for h in range(2):
                for Mt in range(NT):
                    idx = 8 * h + Mt
                    if idx >= 2:
                        tn.wait_ge(S_("dEv"), 2 * idx - 2)
                    if Mt == 1:
                        for ln in range(4):
                            tn.wait_ge(S_(f"dWp{ln}"), 64 * (h + 1))
                    for m in range(NC):
                        if Mt == 0:
                            tn.wait_ge(S_(f"dWp{m % 4}"),
                                       64 * h + 16 * (m // 4 + 1))
                        st, sp = (m == 0), (m == NC - 1)
                        lhs = sT[:, 2048 * Mt + 128 * m:2048 * Mt + 128 * (m + 1)]
                        tn.matmul(pF[2 * (idx % 2)][:, :], lhs,
                                  wpb[:, 1024 * m:1024 * m + 512],
                                  start=st, stop=sp)
                        mm = tn.matmul(pF[2 * (idx % 2) + 1][:, :], lhs,
                                       wpb[:, 1024 * m + 512:1024 * (m + 1)],
                                       start=st, stop=sp)
                    inc(mm, "dMM")
                    for fidx in (2 * idx, 2 * idx + 1):
                        _fT(fidx)
            def _gTp(Mt):
                tn.wait_ge(S_("gGx"), Mt + 1)
                if Mt == 0:
                    tn.wait_ge(S_("fFe"), 64)
                else:
                    tn.wait_ge(S_("gXq"), 4 * Mt)
                for bb in range(4):
                    for j in range(4):
                        c = 4 * bb + j
                        tr = tn.transpose(pH[bb][:, 128 * j:128 * (j + 1)],
                                          qx[:, 128 * c:128 * (c + 1)],
                                          ident[:, :])
                    inc(tr, "gTp")

            for Mt in range(NT):
                if Mt == 0:
                    _gTp(0)
                    tn.wait_ge(S_("dEv"), 32)
                for u in range(4):
                    if Mt >= 1:
                        if u < 2:
                            tn.wait_ge(S_("gP1"), 2 * (Mt - 1) + u + 1)
                        else:
                            tn.wait_ge(S_("gP1V"), 2 * (Mt - 1) + u - 1)
                    for c in range(NC):
                        if u == 0 and c % 4 == 0:
                            tn.wait_ge(S_("gXq"), 4 * Mt + c // 4 + 1)
                        mm = tn.matmul(
                            pF[u][:, :],
                            xqs[Mt % 2][:, 128 * c:128 * (c + 1)],
                            qk[:, 2048 * c + 512 * u:2048 * c + 512 * (u + 1)],
                            start=(c == 0), stop=(c == NC - 1))
                    inc(mm, "gMM")
                    if u == 1 and Mt + 1 < NT:
                        _gTp(Mt + 1)

        hs0 = {}

        def _dEvac(v, idx):
            h, Mt = idx // 8, idx % 8
            v.wait_ge(S_("dMM"), idx + 1)
            _xpos = {2: 1, 4: 2, 6: 3, 0: 4, 3: 1, 5: 2, 7: 3, 1: 4}[Mt]
            v.wait_ge(S_(f"dXr{Mt % 2}"), 16 * _xpos)
            h_ev = None
            for uu in range(2):
                c0 = 2048 * Mt + 512 * (2 * h + uu)
                h_ev = inc(v.tensor_tensor(xr[:, c0:c0 + 512], xr[:, c0:c0 + 512],
                                           pF[2 * (idx % 2) + uu][:, :], ALU.add),
                           "dEv")
            if h == 0:
                xh = xr[:, 2048 * Mt:2048 * Mt + 1024]
                wt(v, h_ev)
                h_mx0 = inc(v.reduce_max(cols[:, 32 + Mt:33 + Mt], xh, AX.X),
                            "selfV")
                h_mn0 = inc(v.tensor_reduce(mn0buf[:, Mt:Mt + 1], xh, AX.X,
                                            ALU.min), "selfV")
                hs0[Mt] = (h_mx0, h_mn0)

        def _eStats1(v, Mt):
            b = 16 * (Mt % 2)
            xrt = xr[:, 2048 * Mt:2048 * (Mt + 1)]
            v.wait_ge(S_("dEv"), 18 + 2 * Mt)
            if Mt >= 2:
                v.wait_ge(S_("gSr"), Mt - 1)
            h_x1 = inc(v.reduce_max(col(b + 5), xrt[:, 1024:2048], AX.X),
                       "selfV")
            h_n1 = inc(v.tensor_reduce(col(b + 6), xrt[:, 1024:2048], AX.X,
                                       ALU.min), "selfV")
            wt(v, hs0[Mt][0])
            wt(v, h_x1)
            h_mx = inc(v.tensor_tensor(cols[:, 48 + Mt:49 + Mt],
                                       cols[:, 32 + Mt:33 + Mt], col(b + 5),
                                       ALU.max), "selfV")
            wt(v, hs0[Mt][1])
            wt(v, h_n1)
            h_mn = inc(v.tensor_tensor(cols[:, 56 + Mt:57 + Mt],
                                       mn0buf[:, Mt:Mt + 1], col(b + 6),
                                       ALU.min), "selfV")
            h_bn = None
            for q4 in range(4):
                h_bn = inc(v.bn_stats(bnbuf[:, 6 * q4:6 * q4 + 6],
                                      xrt[:, 512 * q4:512 * (q4 + 1)]), "selfV")
            wt(v, h_bn)
            mv = bnbuf[:, 0:24].rearrange("p (g c) -> p c g", g=8, c=3)
            h_sm = inc(v.reduce_sum(cols[:, 32 + Mt:33 + Mt], mv[:, 1, :], AX.X),
                       "selfV")
            h_sq = inc(v.tensor_tensor(cols[:, 80:88], mv[:, 1, :], mv[:, 1, :],
                                       ALU.mult), "selfV")
            h_sv = inc(v.reduce_sum(col(b + 2), mv[:, 2, :], AX.X), "selfV")
            wt(v, h_sq)
            h_s2 = inc(v.reduce_sum(col(b + 4), cols[:, 80:88], AX.X), "selfV")
            wt(v, h_sv)
            h_cv = inc(v.tensor_scalar(col(b + 2), col(b + 2), 1.0 / 2048, EPS,
                                       ALU.mult, ALU.add), "selfV")
            wt(v, h_s2)
            wt(v, h_cv)
            h_ex = inc(v.scalar_tensor_tensor(col(b + 4), col(b + 4), 0.125,
                                              col(b + 2), ALU.mult, ALU.add),
                       "selfV")
            wt(v, h_sm)
            h_m = inc(v.tensor_scalar(cols[:, 32 + Mt:33 + Mt],
                                      cols[:, 32 + Mt:33 + Mt], 0.125, None,
                                      ALU.mult), "selfV")
            wt(v, h_m)
            h_m2 = inc(v.tensor_tensor(col(b + 2), cols[:, 32 + Mt:33 + Mt],
                                       cols[:, 32 + Mt:33 + Mt], ALU.mult),
                       "selfV")
            wt(v, h_m2)
            wt(v, h_ex)
            inc(v.tensor_tensor(col(b + 4), col(b + 4), col(b + 2),
                                ALU.subtract), "gVv")
            return (h_mx, h_mn)

        @blk.vector
        def _(v):
            hs = {}
            hmag = {}

            def _magic(t):
                v.wait_ge(S_("gU2"), t + 1)
                hmag[t] = inc(v.tensor_scalar_add(xu2[:, :], xu2[:, :], MAGIC),
                              "selfV")

            def _gXiC(t):
                if t >= 1:
                    v.wait_ge(S_("gGx"), t)
                else:
                    v.wait_ge(S_("dMM"), 12)
                wt(v, hmag[t])
                inc(v.tensor_scalar(xi32[:, :], xu2[:, :], MAGIC, None,
                                    ALU.subtract), "gXi")

            def _part2(t):
                b = 16 * (t % 2)
                v.wait_ge(S_("gSr"), t + 1)
                wt(v, hs[t][0])
                wt(v, hs[t][1])
                h_a = inc(v.tensor_tensor(col(b + 7), cols[:, 48 + t:49 + t],
                                          cols[:, 32 + t:33 + t],
                                          ALU.subtract), "selfV")
                h_b = inc(v.tensor_tensor(col(b + 8), cols[:, 32 + t:33 + t],
                                          cols[:, 56 + t:57 + t],
                                          ALU.subtract), "selfV")
                wt(v, h_a)
                wt(v, h_b)
                h_c = inc(v.tensor_tensor(col(b + 9), col(b + 7), col(b + 8),
                                          ALU.max), "selfV")
                wt(v, h_c)
                h_d = inc(v.tensor_tensor(col(b + 9), col(b + 9),
                                          cols[:, 40 + t:41 + t],
                                          ALU.mult), "selfV")
                wt(v, h_d)
                h_e = inc(v.tensor_scalar(col(b + 10), col(b + 9), Q, GUARD,
                                          ALU.mult, ALU.max), "selfV")
                wt(v, h_e)
                h_f = inc(v.reciprocal(col(b + 11), col(b + 10)), "selfV")
                h_i = inc(v.tensor_copy(cols[:, 48 + t:49 + t], col(b + 10)),
                          "selfV")
                wt(v, h_f)
                h_g = inc(v.tensor_tensor(cols[:, 40 + t:41 + t],
                                          cols[:, 40 + t:41 + t],
                                          col(b + 11), ALU.mult), "selfV")
                wt(v, h_g)
                wt(v, h_i)
                inc(v.scalar_tensor_tensor(cols[:, 32 + t:33 + t],
                                           cols[:, 32 + t:33 + t], -1.0,
                                           cols[:, 40 + t:41 + t], ALU.mult,
                                           ALU.mult), "gVb")

            for i in range(2 * NC):
                ft = i // 2
                if i >= 4 and i % 2 == 0:
                    _dEvac(v, (i - 4) // 2)
                    if i >= 20:
                        hs[(i - 20) // 2] = _eStats1(v, (i - 20) // 2)
                v.wait_ge(S_(f"fKt2{i % 2}"), 16 * (i // 2 + 1))
                wt(v, h_sk[ft])
                h_qb = inc(v.tensor_scalar(ktb[i % 2][:, :], ktb[i % 2][:, :],
                                           cols[:, 64 + ft:65 + ft], MAGIC,
                                           ALU.mult, ALU.add), "selfV")
                wt(v, h_qb)
                h_kv = inc(v.tensor_scalar(kti[i % 2][:, :], ktb[i % 2][:, :],
                                           MAGIC, None, ALU.subtract), "fKv")
                if i >= 3:
                    v.wait_ge(S_("fFt"), 2 * i - 4)
                wt(v, h_kv)
                inc(v.tensor_scalar_mul(kqb[i % 3][:, :], kti[i % 2][:, :],
                                        cols[:, 48 + ft:49 + ft]), "fKg")
                if i == 24:
                    _part2(0)
                elif i == 26:
                    _part2(1)
            for j in range(14, 16):
                _dEvac(v, j)

            for Mt in range(NT + 2):
                if Mt < NT:
                    if Mt >= 1:
                        if Mt + 1 < NT:
                            _part2(Mt + 1)
                        _magic(Mt)
                        _gXiC(Mt)
                    if Mt == 0:
                        _magic(0)
                        _gXiC(0)
                        hs[6] = _eStats1(v, 6)
                        hs[7] = _eStats1(v, 7)
                        v.wait_ge(S_("dMM"), 16)
                        v.wait_ge(S_("gBb"), 16)
                        h1 = inc(v.reduce_max(col(81), bias_b[:, :], AX.X,
                                              apply_absolute_value=True), "selfV")
                        wt(v, h1)
                        h2 = inc(v.tensor_scalar(col(82), col(81), Q, GUARD, ALU.mult,
                                                 ALU.max), "selfV")
                        wt(v, h2)
                        h3 = inc(v.reciprocal(col(83), col(82)), "selfV")
                        wt(v, h3)
                        h4 = inc(v.tensor_scalar(junkG2[:, :], bias_b[:, :], col(83), MAGIC,
                                                 ALU.mult, ALU.add), "selfV")
                        wt(v, h4)
                        h5 = inc(v.tensor_scalar(bqi[:, :], junkG2[:, :], MAGIC, None,
                                                 ALU.subtract), "selfV")
                        wt(v, h5)
                        inc(v.tensor_scalar(bqb[:, :], bqi[:, :], col(82), None, ALU.mult),
                            "gBq")

                k = Mt - 2
                if k == NT - 1:
                    b = 16 * (k % 2)
                    v.wait_ge(S_("gP1"), 2 * k + 2)
                    h_y0 = inc(v.tensor_tensor(ttile[:, 0:1024], ttile[:, 0:1024],
                                               bqb[:, 0:1024], ALU.add), "selfV")
                    wt(v, h_y0)
                    h_y0m = inc(v.reduce_max(col(b + 13), ttile[:, 0:1024],
                                             AX.X), "selfV")
                    for u in range(2, 4):
                        v.wait_ge(S_("gMM"), 4 * k + u + 1)
                        inc(v.tensor_scalar(ttile[:, 512 * u:512 * (u + 1)],
                                            pF[u][:, :], cols[:, 48 + k:49 + k],
                                            None, ALU.mult), "gP1V")
                    v.wait_ge(S_("gP1V"), 2 * k + 2)
                    h_y1 = inc(v.tensor_tensor(ttile[:, 1024:2048],
                                               ttile[:, 1024:2048],
                                               bqb[:, 1024:2048], ALU.add),
                               "selfV")
                    wt(v, h_y1)
                    h_y1m = inc(v.reduce_max(col(b + 14), ttile[:, 1024:2048],
                                             AX.X), "selfV")
                    wt(v, h_y0m)
                    wt(v, h_y1m)
                    h_ym = inc(v.tensor_tensor(col(b + 13), col(b + 13),
                                               col(b + 14), ALU.max), "selfV")
                    wt(v, h_ym)
                    h_c0 = inc(v.tensor_single_scalar(col(b + 13), col(b + 13),
                                                      0.0, ALU.max), "selfV")
                    wt(v, h_c0)
                    h_sy = inc(v.tensor_scalar(cols[:, 56 + k:57 + k], col(b + 13),
                                               Q, GUARD, ALU.mult, ALU.max),
                               "selfV")
                    wt(v, h_sy)
                    h_ry = inc(v.reciprocal(cols[:, 64 + k:65 + k],
                                            cols[:, 56 + k:57 + k]), "selfV")
                    v.wait_ge(S_("gOutR"), k)
                    wt(v, h_ry)
                    for hh2 in range(2):
                        cs = slice(1024 * hh2, 1024 * (hh2 + 1))
                        h_z = inc(v.tensor_scalar(junkG2[:, cs], ttile[:, cs],
                                                  cols[:, 64 + k:65 + k],
                                                  MAGIC, ALU.mult, ALU.add),
                                  "gVz")
                        wt(v, h_z)
                        h_yi = inc(v.tensor_scalar(itile[:, cs],
                                                   junkG2[:, cs], MAGIC,
                                                   None, ALU.subtract),
                                   "selfV")
                        wt(v, h_yi)
                        inc(v.tensor_scalar(junkG2[:, cs], itile[:, cs],
                                            cols[:, 56 + k:57 + k], 0.0,
                                            ALU.mult, ALU.max), "gV1h")
                elif k >= 0:
                    b = 16 * (k % 2)
                    for u in range(2, 4):
                        v.wait_ge(S_("gMM"), 4 * k + u + 1)
                        inc(v.tensor_scalar(ttile[:, 512 * u:512 * (u + 1)],
                                            pF[u][:, :], cols[:, 48 + k:49 + k],
                                            None, ALU.mult), "gP1V")
                    v.wait_ge(S_("gP1"), 2 * k + 2)
                    v.wait_ge(S_("gP1V"), 2 * k + 2)
                    h_y = inc(v.tensor_tensor(ttile[:, :], ttile[:, :],
                                              bqb[:, :], ALU.add), "selfV")
                    wt(v, h_y)
                    h_ym = inc(v.reduce_max(col(b + 13), ttile[:, :], AX.X),
                               "selfV")
                    wt(v, h_ym)
                    h_c0 = inc(v.tensor_single_scalar(col(b + 13), col(b + 13),
                                                      0.0, ALU.max), "selfV")
                    wt(v, h_c0)
                    h_sy = inc(v.tensor_scalar(cols[:, 56 + k:57 + k], col(b + 13),
                                               Q, GUARD, ALU.mult, ALU.max),
                               "selfV")
                    wt(v, h_sy)
                    h_ry = inc(v.reciprocal(cols[:, 64 + k:65 + k],
                                            cols[:, 56 + k:57 + k]), "selfV")
                    if k >= 1:
                        v.wait_ge(S_("gOutR"), k)
                    wt(v, h_ry)
                    h_z = inc(v.tensor_scalar(junkG2[:, :], ttile[:, :],
                                              cols[:, 64 + k:65 + k], MAGIC,
                                              ALU.mult, ALU.add), "gVz")
                    wt(v, h_z)
                    h_yi = inc(v.tensor_scalar(itile[:, :], junkG2[:, :],
                                               MAGIC, None, ALU.subtract),
                               "selfV")
                    wt(v, h_yi)
                    inc(v.tensor_scalar(junkG2[:, :], itile[:, :],
                                        cols[:, 56 + k:57 + k], 0.0,
                                        ALU.mult, ALU.max), "gV1")


        def _sLnexp(s, Mt):
            b = 16 * (Mt % 2)
            s.wait_ge(S_("gVv"), Mt + 1)
            h_ln = inc(s.activation(col(b + 2), col(b + 4), AF.Ln), "selfS")
            wt(s, h_ln)
            inc(s.activation(cols[:, 40 + Mt:41 + Mt], col(b + 2), AF.Exp,
                             scale=-0.5), "gSr")

        @blk.scalar
        def _(s):
            def _u2(t):
                s.wait_ge(S_("gVb"), t + 1)
                if t == 0:
                    s.wait_ge(S_("dMM"), 10)
                else:
                    s.wait_ge(S_("gXi"), t)
                inc(s.activation(xu2[:, :], xr[:, 2048 * t:2048 * (t + 1)],
                                 AF.Identity, bias=cols[:, 32 + t:33 + t],
                                 scale=cols[:, 40 + t:41 + t]), "gU2")

            def _gGx(t):
                s.wait_ge(S_("gXi"), t + 1)
                if t >= 1:
                    s.wait_ge(S_("gTp"), 4 * t)
                else:
                    s.wait_ge(S_("dMM"), 13)
                inc(s.copy(qx[:, :], xi32[:, :]), "gGx")

            for fidx in range(2 * NC):
                ft, hh = fidx // 2, fidx % 2
                for b2 in range(2):
                    slot = (2 * fidx + b2) % 4
                    s.wait_ge(S_("fFt"), 2 * fidx + b2 + 1)
                    dst = qk.rearrange("p (c f) -> p c f", c=NC)[
                        :, 8 * hh + 4 * b2:8 * hh + 4 * b2 + 4,
                        128 * ft:128 * (ft + 1)]
                    src = pH[slot][:, :].rearrange("p (c f) -> p c f", c=4)
                    inc(ecopy(s, dst, src), "fFe")
                if fidx == 28:
                    _u2(0)
                if fidx % 2 == 0 and 18 <= fidx <= 28:
                    _sLnexp(s, (fidx - 18) // 2)
            for Mt in range(NT + 1):
                if Mt < NT:
                    _gGx(Mt)
                    if Mt + 1 < NT:
                        _u2(Mt + 1)
                    if Mt in (0, 1):
                        _sLnexp(s, 6 + Mt)
                    if Mt >= 2:
                        s.wait_ge(S_("gMM"), 4 * Mt - 4)
                    for bb in range(4):
                        s.wait_ge(S_("gTp"), 4 * Mt + bb + 1)
                        if Mt == 0 and bb == 0:
                            s.wait_ge(S_("dMM"), 16)
                        inc(s.copy(xqs[Mt % 2][:, 512 * bb:512 * (bb + 1)],
                                   pH[bb][:, :]), "gXq")
                k = Mt - 1
                if k >= 0:
                    if k == 0:
                        s.wait_ge(S_("gBq"), 1)
                    else:
                        s.wait_ge(S_("gVz"), k)
                    for u in range(2):
                        s.wait_ge(S_("gMM"), 4 * k + u + 1)
                        inc(s.activation(ttile[:, 512 * u:512 * (u + 1)],
                                         pF[u][:, :], AF.Copy,
                                         scale=cols[:, 48 + k:49 + k]), "gP1")

    if PHASES < 4:
        with nc.Block() as blk:
            @blk.sync
            def _(sync):
                for Mt in range(NT):
                    inc(sync.dma_start(d_out[128 * Mt:128 * (Mt + 1), :],
                                       xr[:, 2048 * Mt:2048 * (Mt + 1)]),
                        "zOut", 16)
                sync.wait_ge(S_("zOut"), 16 * NT)
    return nc


_CACHE = {}


def _host_inputs(L0_attn_x, L0_attn_w_qkv, L0_attn_w_proj, L0_ffn_kernel,
                 L0_ffn_bias):
    x = np.ascontiguousarray(np.asarray(L0_attn_x, dtype=np.float32)[0])
    WqkvT = np.ascontiguousarray(np.asarray(L0_attn_w_qkv, dtype=np.float32).T)
    WpT = np.ascontiguousarray(np.asarray(L0_attn_w_proj, dtype=np.float32).T)
    WkT = np.ascontiguousarray(np.asarray(L0_ffn_kernel, dtype=np.float32).T)
    biasr = np.ascontiguousarray(
        np.asarray(L0_ffn_bias, dtype=np.float32).reshape(1, D))
    ident = np.eye(128, dtype=np.float32).astype(BF)
    m = np.zeros((128, 128), dtype=np.float32)
    for t in range(8):
        m[t * 16:(t + 1) * 16, t * 16:(t + 1) * 16] = 1.0
    mask4 = np.ascontiguousarray(np.tile(m, (1, 4))).astype(BF)
    g = np.arange(64)
    h = np.arange(16)
    in_maps, rows_all = [], []
    for k in range(8):
        rows = (h[None, :] * 512 + 64 * k + g[:, None]).reshape(-1)
        rows_all.append(rows)
        in_maps.append({
            "x_att": np.ascontiguousarray(x[1024 * k:1024 * (k + 1)]),
            "x_res": np.ascontiguousarray(x[rows]),
            "wqkvT": WqkvT, "wpT": WpT, "wkT": WkT, "bias": biasr,
            "ident": ident, "mask": mask4,
        })
    return in_maps, rows_all


def kernel(L0_attn_x, L0_attn_w_qkv, L0_attn_w_proj, L0_ffn_kernel, L0_ffn_bias):
    in_maps, rows_all = _host_inputs(L0_attn_x, L0_attn_w_qkv, L0_attn_w_proj,
                                     L0_ffn_kernel, L0_ffn_bias)
    key = PHASES
    if key not in _CACHE:
        _CACHE[key] = build_program()
    res = run_bass_kernel_spmd(_CACHE[key], in_maps, core_ids=list(range(8)))
    out = np.empty((1, 8192, 2048), dtype=np.float32)
    for k in range(8):
        out[0, rows_all[k]] = res.results[k]["out"]
    return out
